# version 29
# baseline (speedup 1.0000x reference)
import numpy as np
import concourse.bass as bass
import concourse.mybir as mybir
from concourse.bass_utils import run_bass_kernel_spmd

F32 = mybir.dt.float32
BF16 = mybir.dt.bfloat16
F16 = mybir.dt.float16
AF = mybir.ActivationFunctionType
ALU = mybir.AluOpType

D, H = 1024, 1536
ND, NH = 8, 12
N_CORES = 8

ENGINES = ("tensor", "scalar", "vector", "gpsimd", "sync")
_LAST_PLOG = {}


class Tr:
    __slots__ = ("w", "r", "psum")

    def __init__(self, psum=False):
        self.w = None
        self.r = []
        self.psum = psum


class Prog:

    def __init__(self):
        self.q = {e: [] for e in ENGINES}
        self.ctr = {}
        self.synced = {e: {} for e in ENGINES}
        self.scope = ""
        self.log = {}

    def _deps(self, reads, writes):
        deps = {}

        def add(tok):
            if tok is None:
                return
            s, v = tok
            if deps.get(s, 0) < v:
                deps[s] = v

        for t in reads:
            add(t.w)
        for t in writes:
            add(t.w)
            for tok in t.r:
                add(tok)
        return deps

    def op(self, eng, fn, reads=(), writes=(), sem=None, inc=1):
        deps = self._deps(reads, writes)
        if eng in ("scalar", "vector"):
            other = "vector" if eng == "scalar" else "scalar"
            for t in reads:
                if getattr(t, "psum", False):
                    for (s0, v0) in t.r:
                        if s0 == other and deps.get(s0, 0) < v0:
                            deps[s0] = v0
        if inc == 16 and sem is not None:
            prev = self.ctr.get(sem, 0)
            if prev and deps.get(sem, 0) < prev:
                deps[sem] = prev
        waits = []
        for s in sorted(deps):
            v = deps[s]
            if eng == "tensor" and s == "tensor":
                continue
            if self.synced[eng].get(s, 0) < v:
                waits.append((s, v))
                self.synced[eng][s] = v
        s = sem if sem is not None else eng
        self.ctr[s] = self.ctr.get(s, 0) + inc
        tok = (s, self.ctr[s])
        self.log[tok] = self.scope
        self.q[eng].append((waits, fn, s, inc))
        for t in reads:
            t.r.append(tok)
        for t in writes:
            t.w = tok
            t.r = []
        return tok

    def wait(self, eng, sem, val):
        self.q[eng].append(([(sem, val)], None, None, 0))
        self.synced[eng][sem] = max(self.synced[eng].get(sem, 0), val)

    def emit(self, block, sems):
        for eng in ENGINES:
            entries = self.q[eng]

            def make_body(entries=entries):
                def body(e):
                    for waits, fn, s, inc in entries:
                        for ws, wv in waits:
                            e.wait_ge(sems[ws], wv)
                        if fn is not None:
                            ins = fn(e)
                            ins.then_inc(sems[s], inc)
                return body

            getattr(block, eng)(make_body())


def build_program(S, T):
    NCH = S // T
    assert S % T == 0
    nc = bass.Bass("TRN2", target_bir_lowering=False)

    x_d = nc.dram_tensor("x", [D, S], F32, kind="ExternalInput")
    out_d = nc.dram_tensor("out", [D, S], F32, kind="ExternalOutput")
    cvec_d = nc.dram_tensor("cvec", [128, 112], F32, kind="ExternalInput")
    w_in = {}
    w_scr = {}
    W_GEOM = {
        "wi": (NH, D), "wr": (ND, D), "wg": (NH, D),
        "lwi": (NH, H), "lwg": (NH, H),
        "wk": (ND, D), "wv": (ND, D), "wo": (ND, H),
    }
    for nm, (nb, K) in W_GEOM.items():
        w_in[nm] = nc.dram_tensor(nm, [nb * 128, K], F32, kind="ExternalInput")
        w_scr[nm] = nc.dram_tensor(nm + "_s", [nb * 128, K], BF16, kind="Internal")

    from contextlib import ExitStack
    stack = ExitStack()

    def sb(name, shape, dt):
        return stack.enter_context(nc.sbuf_tensor(name, shape, dt))

    x_sb = sb("x_sb", [128, 2 * ND * T], F32)
    xn_bf = sb("xn_bf", [128, ND * (T + 1)], BF16)
    u_bf = sb("u_bf", [128, NH * (T + 3)], BF16)
    y_bf = sb("y_bf", [128, NH * T], BF16)
    gg_bf = sb("gg_bf", [128, NH * T], BF16)
    ti_hf = sb("ti_hf", [128, NH * T], F16)
    trg_hf = sb("trg_hf", [128, NH * T], F16)
    tf_hf = sb("tf_hf", [128, ND * T], F16)
    xr_bf = sb("xr_bf", [128, ND * T], BF16)
    xk_bf = sb("xk_bf", [128, ND * T], BF16)
    ka_bf = sb("ka_bf", [128, ND * T], BF16)
    alpha2 = sb("alpha2", [128, 3 * T], F32)
    xbh2 = sb("xbh2", [128, 2 * T], F32)
    beta2 = sb("beta2", [128, 2 * T], F32)
    igs2 = sb("igs2", [128, 3 * T], F32)
    scr2 = sb("scr2", [128, 2 * T], F32)
    diff2 = sb("diff2", [128, 2 * T], F32)
    xbf4 = sb("xbf4", [128, 3 * T], BF16)
    sq4 = sb("sq4", [128, 3 * T], BF16)
    rows = sb("rows", [1, 4 * T], F32)
    cv_sb = sb("cv_sb", [128, 112], F32)
    s_half = sb("s_half", [128, NH], F32)
    bi_h = sb("bi_h", [128, NH], F32)
    bg_h = sb("bg_h", [128, NH], F32)
    hc = sb("hc", [128, NH], F32)
    ones_bf = sb("ones_bf", [128, 1], BF16)
    ones_r = sb("ones_r", [1, 128], F32)
    cb = sb("cb", [128, 3], F32)
    rb_sb = sb("rb_sb", [128, T], F32)
    RING = 9
    ring = sb("ring", [128, RING * H], BF16)
    ring_f32 = ring.bitcast(F32)

    psum = [stack.enter_context(nc.psum_tensor(f"pb{i}", [128, T], F32))
            for i in range(8)]

    sem_names = (["tensor", "scalar", "vector", "gpsimd", "cd0", "cd1", "cd2", "cd3"]
                 + [f"wd{s}" for s in range(RING)] + [f"sd{s}" for s in range(RING)]
                 + [f"xd{j}" for j in range(ND)] + [f"od{j}" for j in range(ND)])
    sems = {n: stack.enter_context(nc.semaphore("sem_" + n)) for n in sem_names}

    P = Prog()
    global _LAST_PLOG
    _LAST_PLOG = P.log

    t_x = [[Tr() for _ in range(ND)], [Tr() for _ in range(ND)]]
    t_xn = [Tr() for _ in range(ND)]
    t_u = [Tr() for _ in range(NH)]
    t_y = [Tr() for _ in range(NH)]
    t_gg = [Tr() for _ in range(NH)]
    t_ti = [Tr() for _ in range(NH)]
    t_trg = [Tr() for _ in range(NH)]
    t_tf = [Tr() for _ in range(ND)]
    t_xr = [Tr() for _ in range(ND)]
    t_xk = [Tr() for _ in range(ND)]
    t_ka = [Tr() for _ in range(ND)]
    t_alpha = [Tr() for _ in range(3)]
    t_xbh = [Tr() for _ in range(2)]
    t_beta = [Tr() for _ in range(2)]
    t_igs = [Tr() for _ in range(3)]
    t_s2 = [Tr() for _ in range(2)]
    t_diff = [Tr() for _ in range(2)]
    t_xbf = [Tr() for _ in range(3)]
    t_sq = [Tr() for _ in range(3)]
    t_rows = Tr()
    t_cv = Tr()
    t_sh = Tr()
    t_bih = Tr()
    t_bgh = Tr()
    t_hc = Tr()
    t_ones = Tr()
    t_onesr = Tr()
    t_ring = [Tr() for _ in range(RING)]
    t_cvt = Tr()
    t_ps = [Tr(psum=True) for _ in range(8)]
    t_scr = {nm: [Tr() for _ in range(nb)] for nm, (nb, K) in W_GEOM.items()}

    def xv(cc, j):
        b = ((cc % 2) * ND + j) * T
        return x_sb[:, b:b + T]

    def xnv(j, lo, hi):
        b = j * (T + 1)
        return xn_bf[:, b + lo: b + hi]

    def uv(j, lo, hi):
        b = j * (T + 3)
        return u_bf[:, b + lo: b + hi]

    def tile(t_buf, j):
        return t_buf[:, j * T:(j + 1) * T]

    def rot(buf, i):
        return buf[:, (i % 2) * T:((i % 2) + 1) * T]

    def rot4(buf, i):
        return buf[:, (i % 4) * T:((i % 4) + 1) * T]

    def rot3(buf, i):
        return buf[:, (i % 3) * T:((i % 3) + 1) * T]

    mean_row = rows[0:1, 0:T]
    msq_row = rows[0:1, T:2 * T]
    var_row = rows[0:1, 2 * T:3 * T]
    rstd_row = rows[0:1, 3 * T:4 * T]
    t_mean = Tr(); t_msq = Tr(); t_var = Tr(); t_rstd = Tr()
    t_rb = Tr()

    def ring_view(s, K):
        return ring[:, s * H: s * H + K]

    ring_state = {"i": 0, "q": 0}

    def next_slot():
        s = ring_state["i"] % RING
        ring_state["i"] += 1
        return s

    cvst = {"q": 0, "d": 0, "st": 0}

    def fetch_weight(nm, j, c):
        nb, K = W_GEOM[nm]
        if c == 0:
            q = cvst["q"] % 3
            cvst["q"] += 1
            s = 6 + cvst["d"] % 3
            cvst["d"] += 1
            stage = ring_f32[:, q * H:q * H + K]
            rsv = ring_view(s, K)
            P.op("sync",
                 lambda e, o=stage, ap=w_in[nm].ap()[j * 128:(j + 1) * 128, :]:
                 e.dma_start(o, ap),
                 reads=[], writes=[t_ring[2 * q], t_ring[2 * q + 1]],
                 sem=f"cd{q}", inc=16)
            eng = ("vector", "gpsimd")[(j + ord(nm[1])) % 2]
            P.op(eng, lambda e, o=rsv, i=stage: e.tensor_copy(o, i),
                 reads=[t_ring[2 * q], t_ring[2 * q + 1]], writes=[t_ring[s]])
            P.op("gpsimd",
                 lambda e, o=w_scr[nm].ap()[j * 128:(j + 1) * 128, :], i=rsv:
                 e.dma_start(o, i),
                 reads=[t_ring[s]], writes=[t_scr[nm][j]], sem=f"sd{s}", inc=16)
        else:
            s = next_slot()
            rsv = ring_view(s, K)
            P.op("sync",
                 lambda e, o=rsv, ap=w_scr[nm].ap()[j * 128:(j + 1) * 128, :]:
                 e.dma_start(o, ap),
                 reads=[t_scr[nm][j]], writes=[t_ring[s]], sem=f"wd{s}", inc=16)
        return s

    def mm_group(ps_i, slot, K, rhs_views, rhs_trs, extra_reads=()):
        nk = K // 128
        rsv = ring_view(slot, K)
        for kh in range(nk):
            def fn(e, rsv=rsv, kh=kh, v=rhs_views[kh], p=psum[ps_i],
                   st=(kh == 0), sp=(kh == nk - 1)):
                return e.matmul(p[:, :], rsv[:, kh * 128:(kh + 1) * 128], v,
                                start=st, stop=sp)
            extra = list(extra_reads) if kh == 0 else []
            P.op("tensor", fn,
                 reads=[t_ring[slot], rhs_trs[kh]] + extra,
                 writes=[t_ps[ps_i]])

    P.op("sync", lambda e: e.dma_start(cv_sb[:], cvec_d.ap()[:, :]),
         reads=[], writes=[t_cv], sem="cd0", inc=16)
    for j in range(ND):
        P.op("sync",
             lambda e, o=xv(0, j), ap=x_d.ap()[j * 128:(j + 1) * 128, 0:T]:
             e.dma_start(o, ap),
             reads=[], writes=[t_x[0][j]], sem=f"xd{j}", inc=16)
    P.op("vector", lambda e: e.memset(ones_bf[:], 1.0), writes=[t_ones])
    P.op("vector", lambda e: e.memset(ones_r[:], 1.0), writes=[t_onesr])
    t_cb = Tr()
    P.op("vector", lambda e: e.memset(cb[:, 0:1], 1.000001), writes=[t_cb])
    P.op("vector", lambda e: e.memset(cb[:, 1:2], 1e-5), writes=[t_cb])
    P.op("vector", lambda e: e.memset(cb[:, 2:3], -0.6931471805599453), writes=[t_cb])
    P.op("vector", lambda e: e.memset(hc[:], 0.0), writes=[t_hc])
    P.op("vector", lambda e: e.memset(xn_bf[:], 0.0), writes=t_xn)
    P.op("vector", lambda e: e.memset(u_bf[:], 0.0), writes=t_u)
    lam_v = cv_sb[:, 60:72]
    P.op("scalar", lambda e: e.activation(s_half[:], lam_v, AF.Exp),
         reads=[t_cv], writes=[t_sh])
    P.op("scalar", lambda e: e.activation(s_half[:], s_half[:], AF.Ln, bias=1.0),
         reads=[], writes=[t_sh])
    P.op("scalar", lambda e: e.activation(s_half[:], s_half[:], AF.Copy, scale=-4.0),
         reads=[], writes=[t_sh])
    P.op("scalar", lambda e: e.activation(bi_h[:], cv_sb[:, 72:84], AF.Copy, scale=0.5),
         reads=[t_cv], writes=[t_bih])
    P.op("scalar", lambda e: e.activation(bg_h[:], cv_sb[:, 84:96], AF.Copy, scale=0.5),
         reads=[t_cv], writes=[t_bgh])

    xn_views = tuple(xnv(kh, 1, T + 1) for kh in range(ND))
    y_views = tuple(tile(y_bf, kh) for kh in range(NH))
    xr_views = tuple(tile(xr_bf, kh) for kh in range(ND))
    xk_views = tuple(tile(xk_bf, kh) for kh in range(ND))
    ka_views = tuple(tile(ka_bf, kh) for kh in range(ND))
    gh_views = tuple(tile(gg_bf, kh) for kh in range(NH))

    def plan_casts(cc, j):
        P.op("vector", lambda e, o=rot3(xbf4, j), i=xv(cc, j):
             e.tensor_copy(o, i),
             reads=[t_x[cc % 2][j]], writes=[t_xbf[j % 3]])
        P.op("scalar", lambda e, o=rot3(sq4, j), i=xv(cc, j):
             e.activation(o, i, AF.Square),
             reads=[t_x[cc % 2][j]], writes=[t_sq[j % 3]])

    def plan_A(cc):
        P.scope = f"A{cc}"
        """LN stats + normalize + ffn token-shift prep for chunk cc; planned
        before the previous chunk's D stage so it overlaps it."""
        tx = t_x[cc % 2]
        for j in range(ND):
            if j >= 3:
                plan_casts(cc, j)
            P.op("tensor", lambda e, r=rot3(xbf4, j), j=j:
                 e.matmul(psum[0][0:1, :], ones_bf[:, :], r,
                          start=(j == 0), stop=(j == ND - 1)),
                 reads=[t_ones, t_xbf[j % 3]], writes=[t_ps[0]])
            P.op("tensor", lambda e, r=rot3(sq4, j), j=j:
                 e.matmul(psum[1][0:1, :], ones_bf[:, :], r,
                          start=(j == 0), stop=(j == ND - 1)),
                 reads=[t_ones, t_sq[j % 3]], writes=[t_ps[1]])
        P.op("scalar", lambda e: e.activation(mean_row, psum[0][0:1, :],
                                              AF.Copy, scale=1.0 / D),
             reads=[t_ps[0]], writes=[t_mean])
        P.op("scalar", lambda e: e.activation(msq_row, psum[0][0:1, :],
                                              AF.Square, scale=1.0 / D),
             reads=[t_ps[0]], writes=[t_msq])
        P.op("scalar", lambda e: e.activation(var_row, psum[1][0:1, :],
                                              AF.Copy, scale=1.0 / D),
             reads=[t_ps[1]], writes=[t_var])
        P.op("vector", lambda e: e.tensor_sub(var_row, var_row, msq_row),
             reads=[t_msq], writes=[t_var])
        P.op("scalar", lambda e: e.activation(var_row, var_row, AF.Ln,
                                              bias=cb[0:1, 1:2]),
             reads=[t_cb], writes=[t_var])
        P.op("scalar", lambda e: e.activation(rstd_row, var_row, AF.Exp,
                                              scale=-0.5),
             reads=[t_var], writes=[t_rstd])
        P.op("tensor", lambda e: e.matmul(psum[0][:, :], ones_r[:, :],
                                          mean_row, start=True, stop=True),
             reads=[t_onesr, t_mean], writes=[t_ps[0]])
        P.op("tensor", lambda e: e.matmul(psum[1][:, :], ones_r[:, :],
                                          rstd_row, start=True, stop=True),
             reads=[t_onesr, t_rstd], writes=[t_ps[1]])
        P.op("scalar", lambda e: e.activation(rb_sb[:], psum[1][:, :],
                                              AF.Copy),
             reads=[t_ps[1]], writes=[t_rb])
        for j in range(ND):
            P.op("vector", lambda e, o=rot(diff2, j), i=xv(cc, j):
                 e.tensor_sub(o, i, psum[0][:, :]),
                 reads=[tx[j], t_ps[0]], writes=[t_diff[j % 2]])
            if j % 2 == 0:
                P.op("vector", lambda e, o=xnv(j, 1, T + 1), i=rot(diff2, j):
                     e.tensor_mul(o, i, psum[1][:, :]),
                     reads=[t_diff[j % 2], t_ps[1]], writes=[t_xn[j]])
            else:
                P.op("gpsimd", lambda e, o=xnv(j, 1, T + 1), i=rot(diff2, j):
                     e.tensor_mul(o, i, rb_sb[:]),
                     reads=[t_diff[j % 2], t_rb], writes=[t_xn[j]])
        for j in range(ND):
            P.op("vector", lambda e, o=rot(diff2, j), a=xnv(j, 1, T + 1),
                 b=xnv(j, 0, T): e.tensor_sub(o, a, b),
                 reads=[t_xn[j]], writes=[t_diff[j % 2]])
            P.op("vector", lambda e, o=tile(xr_bf, j), d=rot(diff2, j),
                 m=cv_sb[:, 96 + j:97 + j], p=xnv(j, 0, T):
                 e.scalar_tensor_tensor(o, d, m, p, ALU.mult, ALU.add),
                 reads=[t_diff[j % 2], t_cv, t_xn[j]], writes=[t_xr[j]])
            P.op("vector", lambda e, o=tile(xk_bf, j), d=rot(diff2, j),
                 m=cv_sb[:, 104 + j:105 + j], p=xnv(j, 0, T):
                 e.scalar_tensor_tensor(o, d, m, p, ALU.mult, ALU.add),
                 reads=[t_diff[j % 2], t_cv, t_xn[j]], writes=[t_xk[j]])

    def plan_B(c):
        P.scope = f"Bu{c}"
        for j in range(NH):
            pu = (2, 3, 0, 1)[j % 4]
            s = fetch_weight("wi", j, c)
            mm_group(pu, s, D, xn_views, t_xn)
            P.op("scalar", lambda e, o=uv(j, 3, T + 3), p=psum[pu]:
                 e.activation(o, p[:, :], AF.Copy),
                 reads=[t_ps[pu]], writes=[t_u[j]])
        P.scope = f"conv{c}"
        for j in range(NH):
            P.op("vector", lambda e, o=rot(scr2, j), i=uv(j, 0, T),
                 w=cv_sb[:, 4 * j:4 * j + 1], b=cv_sb[:, 48 + j:49 + j]:
                 e.tensor_scalar(o, i, w, b, ALU.mult, ALU.add),
                 reads=[t_u[j], t_cv], writes=[t_s2[j % 2]])
            for tap in (1, 2):
                P.op("vector", lambda e, o=rot(scr2, j), i=uv(j, tap, T + tap),
                     w=cv_sb[:, 4 * j + tap:4 * j + tap + 1], s2=rot(scr2, j):
                     e.scalar_tensor_tensor(o, i, w, s2, ALU.mult, ALU.add),
                     reads=[t_u[j], t_cv], writes=[t_s2[j % 2]])
            P.op("vector", lambda e, o=tile(y_bf, j), i=uv(j, 3, T + 3),
                 w=cv_sb[:, 4 * j + 3:4 * j + 4], s2=rot(scr2, j):
                 e.scalar_tensor_tensor(o, i, w, s2, ALU.mult, ALU.add),
                 reads=[t_u[j], t_cv, t_s2[j % 2]], writes=[t_y[j]])
            P.op("gpsimd", lambda e, o=uv(j, 0, 3), i=uv(j, T, T + 3):
                 e.tensor_copy(o, i),
                 reads=[], writes=[t_u[j]])
        P.scope = f"Bwr{c}"
        for j in range(ND):
            s = fetch_weight("wr", j, c)
            mm_group(4 + j % 2, s, D, xr_views, t_xr)
            P.op("scalar", lambda e, o=tile(tf_hf, j), p=psum[4 + j % 2]:
                 e.activation(o, p[:, :], AF.Tanh, scale=0.5),
                 reads=[t_ps[4 + j % 2]], writes=[t_tf[j]])
        P.scope = f"Bg{c}"
        for j in range(NH):
            s = fetch_weight("wg", j, c)
            mm_group(6 + j % 2, s, D, xn_views, t_xn)
            P.op("scalar", lambda e, o=tile(gg_bf, j), p=psum[6 + j % 2]:
                 e.activation(o, p[:, :], AF.Gelu),
                 reads=[t_ps[6 + j % 2]], writes=[t_gg[j]])
        for j in range(ND):
            P.op("gpsimd", lambda e, o=xnv(j, 0, 1), i=xnv(j, T, T + 1):
                 e.tensor_copy(o, i),
                 reads=[], writes=[t_xn[j]])

    def plan_CE(c, mid=None):
        def plan_c1(j):
            P.scope = f"C1_{c}_{j}"
            pi = (4, 6, 2)[j % 3]
            pr = (5, 7, 3)[j % 3]
            s = fetch_weight("lwi", j, c)
            mm_group(pi, s, H, y_views, t_y)
            P.op("scalar", lambda e, o=tile(ti_hf, j), p=psum[pi],
                 b=bi_h[:, j:j + 1]:
                 e.activation(o, p[:, :], AF.Tanh, bias=b, scale=0.5),
                 reads=[t_ps[pi], t_bih], writes=[t_ti[j]])
            s = fetch_weight("lwg", j, c)
            mm_group(pr, s, H, y_views, t_y)
            P.op("scalar", lambda e, o=tile(trg_hf, j), p=psum[pr],
                 b=bg_h[:, j:j + 1]:
                 e.activation(o, p[:, :], AF.Tanh, bias=b, scale=0.5),
                 reads=[t_ps[pr], t_bgh], writes=[t_trg[j]])

        def plan_c2(j):
            P.scope = f"C2_{c}_{j}"
            sh = s_half[:, j:j + 1]
            P.op("scalar", lambda e, o=rot3(alpha2, j), i=tile(trg_hf, j),
                 sh=sh: e.activation(o, i, AF.Exp, bias=sh, scale=sh),
                 reads=[t_trg[j], t_sh], writes=[t_alpha[j % 3]])
            P.op("scalar", lambda e, o=rot(beta2, j), i=rot3(alpha2, j):
                 e.activation(o, i, AF.Square),
                 reads=[t_alpha[j % 3]], writes=[t_beta[j % 2]])
            P.op("scalar", lambda e, o=rot(beta2, j), i=rot(beta2, j):
                 e.activation(o, i, AF.Ln, bias=cb[:, 0:1], scale=-1.0),
                 reads=[t_cb], writes=[t_beta[j % 2]])
            P.op("scalar", lambda e, o=rot(beta2, j), i=rot(beta2, j):
                 e.activation(o, i, AF.Exp, bias=cb[:, 2:3], scale=0.5),
                 reads=[t_cb], writes=[t_beta[j % 2]])
            P.op("vector", lambda e, o=rot3(igs2, j), i=tile(ti_hf, j),
                 b=rot(beta2, j):
                 e.scalar_tensor_tensor(o, i, 1.0, b, ALU.add, ALU.mult),
                 reads=[t_ti[j], t_beta[j % 2]], writes=[t_igs[j % 3]])
            P.op("gpsimd", lambda e, o=rot(xbh2, j), i=rot3(igs2, j),
                 y=tile(y_bf, j): e.tensor_mul(o, i, y),
                 reads=[t_igs[j % 3], t_y[j]], writes=[t_xbh[j % 2]])
            P.op("vector", lambda e, o=rot3(igs2, j), a=rot3(alpha2, j),
                 b=rot(xbh2, j), i=hc[:, j:j + 1]:
                 e.tensor_tensor_scan(o, a, b, i, ALU.mult, ALU.add),
                 reads=[t_alpha[j % 3], t_xbh[j % 2], t_hc],
                 writes=[t_igs[j % 3]])
            P.op("gpsimd", lambda e, o=hc[:, j:j + 1],
                 i=rot3(igs2, j)[:, T - 1:T]: e.tensor_copy(o, i),
                 reads=[t_igs[j % 3]], writes=[t_hc])
            P.op("gpsimd", lambda e, o=tile(gg_bf, j), h=rot3(igs2, j):
                 e.tensor_mul(o, o, h),
                 reads=[t_igs[j % 3]], writes=[t_gg[j]])

        def plan_e1(j):
            P.scope = f"E1_{c}_{j}"
            pk = (0, 1)[j % 2]
            s = fetch_weight("wk", j, c)
            mm_group(pk, s, D, xk_views, t_xk)
            P.op("scalar", lambda e, o=rot(scr2, j), p=psum[pk]:
                 e.activation(o, p[:, :], AF.Relu),
                 reads=[t_ps[pk]], writes=[t_s2[j % 2]])
            P.op("scalar", lambda e, o=tile(ka_bf, j), i=rot(scr2, j):
                 e.activation(o, i, AF.Square, scale=0.7071067811865476),
                 reads=[t_s2[j % 2]], writes=[t_ka[j]])

        def plan_e2(j):
            P.scope = f"E2_{c}_{j}"
            pv = (2, 3, 0, 1)[j % 4]
            s = fetch_weight("wv", j, c)
            mm_group(pv, s, D, ka_views, t_ka)
            P.op("vector", lambda e, o=rot(scr2, j), i=tile(tf_hf, j),
                 p=psum[pv]:
                 e.scalar_tensor_tensor(o, i, 1.0, p[:, :], ALU.add,
                                        ALU.mult),
                 reads=[t_tf[j], t_ps[pv]], writes=[t_s2[j % 2]])
            P.op("vector", lambda e, o=xv(c, j), i=rot(scr2, j):
                 e.tensor_add(o, o, i),
                 reads=[t_s2[j % 2]], writes=[t_x[c % 2][j]])

        def weave(aa, bb):
            out = []
            na, nb = len(aa), len(bb)
            i = k = 0
            while i < na or k < nb:
                if i < na:
                    out.append(aa[i]); i += 1
                if k < nb:
                    out.append(bb[k]); k += 1
            return out

        for j in range(6):
            plan_c1(j)
        for unit in weave([(plan_e1, j) for j in range(ND)],
                          [(plan_c2, j) for j in range(6)]):
            unit[0](unit[1])
        if mid is not None:
            mid()
        for j in range(6, NH):
            plan_c1(j)
        for unit in weave([(plan_e2, j) for j in range(ND)],
                          [(plan_c2, j) for j in range(6, NH)]):
            unit[0](unit[1])

    def plan_D(c):
        P.scope = f"D{c}"
        col0 = c * T
        for j in range(ND):
            s = fetch_weight("wo", j, c)
            mm_group(4 + j % 4, s, H, gh_views, t_gg)
            P.op("vector", lambda e, o=xv(c, j), p=psum[4 + j % 4]:
                 e.tensor_add(o, o, p[:, :]),
                 reads=[t_ps[4 + j % 4]], writes=[t_x[c % 2][j]])
            P.op("scalar",
                 lambda e, i=xv(c, j), ap=out_d.ap()[j * 128:(j + 1) * 128,
                                                     col0:col0 + T]:
                 e.dma_start(ap, i),
                 reads=[t_x[c % 2][j]], writes=[], sem=f"od{j}", inc=16)

    for j in range(3):
        plan_casts(0, j)
    plan_A(0)

    for c in range(NCH):
        if c + 1 < NCH:
            for j in range(ND):
                P.op("scalar",
                     lambda e, o=xv(c + 1, j),
                     ap=x_d.ap()[j * 128:(j + 1) * 128,
                                 (c + 1) * T:(c + 2) * T]:
                     e.dma_start(o, ap),
                     reads=[], writes=[t_x[(c + 1) % 2][j]],
                     sem=f"xd{j}", inc=16)
        plan_B(c)

        def mid():
            for j in range(3):
                plan_casts(c + 1, j)
            plan_A(c + 1)
        plan_CE(c, mid if c + 1 < NCH else None)
        plan_D(c)

    for j in range(ND):
        P.wait("sync", f"od{j}", 16 * NCH)

    with nc.Block() as block:
        P.emit(block, sems)

    return nc


def _blk(W, nb, K):
    a = np.asarray(W, dtype=np.float32).reshape(nb, 128, K // 128, 128)
    a = a.transpose(0, 3, 2, 1)
    return np.ascontiguousarray(a.reshape(nb * 128, K))


def make_inputs(x_b, Wg, Wi, conv_w, conv_b, lru_Wi, lru_bi, lru_Wg, lru_bg,
                lam, Wo, W_r, W_k, W_v, mu_r, mu_k):
    cv = np.zeros((128, 112), np.float32)
    cv[:, 0:48] = np.asarray(conv_w, np.float32).reshape(NH, 128, 4) \
        .transpose(1, 0, 2).reshape(128, 48)
    cv[:, 48:60] = np.asarray(conv_b, np.float32).reshape(NH, 128).T
    cv[:, 60:72] = np.asarray(lam, np.float32).reshape(NH, 128).T
    cv[:, 72:84] = np.asarray(lru_bi, np.float32).reshape(NH, 128).T
    cv[:, 84:96] = np.asarray(lru_bg, np.float32).reshape(NH, 128).T
    cv[:, 96:104] = np.asarray(mu_r, np.float32).reshape(ND, 128).T
    cv[:, 104:112] = np.asarray(mu_k, np.float32).reshape(ND, 128).T
    shared = {
        "cvec": cv,
        "wg": _blk(Wg, NH, D),
        "wi": _blk(Wi, NH, D),
        "lwi": _blk(lru_Wi, NH, H),
        "lwg": _blk(lru_Wg, NH, H),
        "wo": _blk(Wo, ND, H),
        "wr": _blk(W_r, ND, D),
        "wk": _blk(W_k, ND, D),
        "wv": _blk(W_v, ND, D),
    }
    m = dict(shared)
    m["x"] = np.ascontiguousarray(np.asarray(x_b, np.float32).T)
    return m


def kernel(L0_attn_x, L0_attn_Wg, L0_attn_Wi, L0_attn_conv_w, L0_attn_conv_b,
           L0_attn_lru_Wi, L0_attn_lru_bi, L0_attn_lru_Wg, L0_attn_lru_bg,
           L0_attn_lam, L0_attn_Wo, L0_ffn_W_r, L0_ffn_W_k, L0_ffn_W_v,
           L0_ffn_mu_r, L0_ffn_mu_k):
    x = np.asarray(L0_attn_x, np.float32)
    B, S, _ = x.shape
    nc = build_program(S, 512)
    in_maps = []
    for b in range(B):
        in_maps.append(make_inputs(
            x[b], L0_attn_Wg, L0_attn_Wi, L0_attn_conv_w, L0_attn_conv_b,
            L0_attn_lru_Wi, L0_attn_lru_bi, L0_attn_lru_Wg, L0_attn_lru_bg,
            L0_attn_lam, L0_attn_Wo, L0_ffn_W_r, L0_ffn_W_k, L0_ffn_W_v,
            L0_ffn_mu_r, L0_ffn_mu_k))
    res = run_bass_kernel_spmd(nc, in_maps, core_ids=list(range(N_CORES)))
    out = np.stack([np.ascontiguousarray(np.asarray(r["out"]).T)
                    for r in res.results], axis=0)
    return out.astype(np.float32)


# version 30
# speedup vs baseline: 1.0052x; 1.0052x over previous
import numpy as np
import concourse.bass as bass
import concourse.mybir as mybir
from concourse.bass_utils import run_bass_kernel_spmd

F32 = mybir.dt.float32
BF16 = mybir.dt.bfloat16
F16 = mybir.dt.float16
AF = mybir.ActivationFunctionType
ALU = mybir.AluOpType

D, H = 1024, 1536
ND, NH = 8, 12
N_CORES = 8

ENGINES = ("tensor", "scalar", "vector", "gpsimd", "sync")
_LAST_PLOG = {}


class Tr:
    __slots__ = ("w", "r", "psum")

    def __init__(self, psum=False):
        self.w = None
        self.r = []
        self.psum = psum


class Prog:

    def __init__(self):
        self.q = {e: [] for e in ENGINES}
        self.ctr = {}
        self.synced = {e: {} for e in ENGINES}
        self.scope = ""
        self.log = {}

    def _deps(self, reads, writes):
        deps = {}

        def add(tok):
            if tok is None:
                return
            s, v = tok
            if deps.get(s, 0) < v:
                deps[s] = v

        for t in reads:
            add(t.w)
        for t in writes:
            add(t.w)
            for tok in t.r:
                add(tok)
        return deps

    def op(self, eng, fn, reads=(), writes=(), sem=None, inc=1):
        deps = self._deps(reads, writes)
        if eng in ("scalar", "vector"):
            other = "vector" if eng == "scalar" else "scalar"
            for t in reads:
                if getattr(t, "psum", False):
                    for (s0, v0) in t.r:
                        if s0 == other and deps.get(s0, 0) < v0:
                            deps[s0] = v0
        if inc == 16 and sem is not None:
            prev = self.ctr.get(sem, 0)
            if prev and deps.get(sem, 0) < prev:
                deps[sem] = prev
        waits = []
        for s in sorted(deps):
            v = deps[s]
            if eng == "tensor" and s == "tensor":
                continue
            if self.synced[eng].get(s, 0) < v:
                waits.append((s, v))
                self.synced[eng][s] = v
        s = sem if sem is not None else eng
        self.ctr[s] = self.ctr.get(s, 0) + inc
        tok = (s, self.ctr[s])
        self.log[tok] = self.scope
        self.q[eng].append((waits, fn, s, inc))
        for t in reads:
            t.r.append(tok)
        for t in writes:
            t.w = tok
            t.r = []
        return tok

    def wait(self, eng, sem, val):
        self.q[eng].append(([(sem, val)], None, None, 0))
        self.synced[eng][sem] = max(self.synced[eng].get(sem, 0), val)

    def emit(self, block, sems):
        for eng in ENGINES:
            entries = self.q[eng]

            def make_body(entries=entries):
                def body(e):
                    for waits, fn, s, inc in entries:
                        for ws, wv in waits:
                            e.wait_ge(sems[ws], wv)
                        if fn is not None:
                            ins = fn(e)
                            ins.then_inc(sems[s], inc)
                return body

            getattr(block, eng)(make_body())


def build_program(S, T):
    NCH = S // T
    assert S % T == 0
    nc = bass.Bass("TRN2", target_bir_lowering=False)

    x_d = nc.dram_tensor("x", [D, S], F32, kind="ExternalInput")
    out_d = nc.dram_tensor("out", [D, S], F32, kind="ExternalOutput")
    cvec_d = nc.dram_tensor("cvec", [128, 112], F32, kind="ExternalInput")
    w_in = {}
    w_scr = {}
    W_GEOM = {
        "wi": (NH, D), "wr": (ND, D), "wg": (NH, D),
        "lwi": (NH, H), "lwg": (NH, H),
        "wk": (ND, D), "wv": (ND, D), "wo": (ND, H),
    }
    for nm, (nb, K) in W_GEOM.items():
        w_in[nm] = nc.dram_tensor(nm, [nb * 128, K], F32, kind="ExternalInput")
        w_scr[nm] = nc.dram_tensor(nm + "_s", [nb * 128, K], BF16, kind="Internal")

    from contextlib import ExitStack
    stack = ExitStack()

    def sb(name, shape, dt):
        return stack.enter_context(nc.sbuf_tensor(name, shape, dt))

    x_sb = sb("x_sb", [128, 2 * ND * T], F32)
    xn_bf = sb("xn_bf", [128, ND * (T + 1)], BF16)
    u_bf = sb("u_bf", [128, NH * (T + 3)], BF16)
    y_bf = sb("y_bf", [128, NH * T], BF16)
    gg_bf = sb("gg_bf", [128, NH * T], BF16)
    ti_hf = sb("ti_hf", [128, NH * T], F16)
    trg_hf = sb("trg_hf", [128, NH * T], F16)
    tf_hf = sb("tf_hf", [128, ND * T], F16)
    xr_bf = sb("xr_bf", [128, ND * T], BF16)
    xk_bf = sb("xk_bf", [128, ND * T], BF16)
    ka_bf = sb("ka_bf", [128, ND * T], BF16)
    alpha2 = sb("alpha2", [128, 3 * T], F32)
    xbh2 = sb("xbh2", [128, 2 * T], F32)
    beta2 = sb("beta2", [128, 2 * T], F32)
    igs2 = sb("igs2", [128, 3 * T], F32)
    scr2 = sb("scr2", [128, 2 * T], F32)
    diff2 = sb("diff2", [128, 2 * T], F32)
    xbf4 = sb("xbf4", [128, 3 * T], BF16)
    sq4 = sb("sq4", [128, 3 * T], BF16)
    rows = sb("rows", [1, 4 * T], F32)
    cv_sb = sb("cv_sb", [128, 112], F32)
    s_half = sb("s_half", [128, NH], F32)
    bi_h = sb("bi_h", [128, NH], F32)
    bg_h = sb("bg_h", [128, NH], F32)
    hc = sb("hc", [128, NH], F32)
    ones_bf = sb("ones_bf", [128, 1], BF16)
    ones_r = sb("ones_r", [1, 128], F32)
    cb = sb("cb", [128, 3], F32)
    rb_sb = sb("rb_sb", [128, T], F32)
    RING = 10
    ring = sb("ring", [128, RING * H], BF16)
    ring_f32 = ring.bitcast(F32)

    psum = [stack.enter_context(nc.psum_tensor(f"pb{i}", [128, T], F32))
            for i in range(8)]

    sem_names = (["tensor", "scalar", "vector", "gpsimd", "cd0", "cd1", "cd2", "cd3"]
                 + [f"wd{s}" for s in range(RING)] + [f"sd{s}" for s in range(RING)]
                 + [f"xd{j}" for j in range(ND)] + [f"od{j}" for j in range(ND)])
    sems = {n: stack.enter_context(nc.semaphore("sem_" + n)) for n in sem_names}

    P = Prog()
    global _LAST_PLOG
    _LAST_PLOG = P.log

    t_x = [[Tr() for _ in range(ND)], [Tr() for _ in range(ND)]]
    t_xn = [Tr() for _ in range(ND)]
    t_u = [Tr() for _ in range(NH)]
    t_y = [Tr() for _ in range(NH)]
    t_gg = [Tr() for _ in range(NH)]
    t_ti = [Tr() for _ in range(NH)]
    t_trg = [Tr() for _ in range(NH)]
    t_tf = [Tr() for _ in range(ND)]
    t_xr = [Tr() for _ in range(ND)]
    t_xk = [Tr() for _ in range(ND)]
    t_ka = [Tr() for _ in range(ND)]
    t_alpha = [Tr() for _ in range(3)]
    t_xbh = [Tr() for _ in range(2)]
    t_beta = [Tr() for _ in range(2)]
    t_igs = [Tr() for _ in range(3)]
    t_s2 = [Tr() for _ in range(2)]
    t_diff = [Tr() for _ in range(2)]
    t_xbf = [Tr() for _ in range(3)]
    t_sq = [Tr() for _ in range(3)]
    t_rows = Tr()
    t_cv = Tr()
    t_sh = Tr()
    t_bih = Tr()
    t_bgh = Tr()
    t_hc = Tr()
    t_ones = Tr()
    t_onesr = Tr()
    t_ring = [Tr() for _ in range(RING)]
    t_cvt = Tr()
    t_ps = [Tr(psum=True) for _ in range(8)]
    t_scr = {nm: [Tr() for _ in range(nb)] for nm, (nb, K) in W_GEOM.items()}

    def xv(cc, j):
        b = ((cc % 2) * ND + j) * T
        return x_sb[:, b:b + T]

    def xnv(j, lo, hi):
        b = j * (T + 1)
        return xn_bf[:, b + lo: b + hi]

    def uv(j, lo, hi):
        b = j * (T + 3)
        return u_bf[:, b + lo: b + hi]

    def tile(t_buf, j):
        return t_buf[:, j * T:(j + 1) * T]

    def rot(buf, i):
        return buf[:, (i % 2) * T:((i % 2) + 1) * T]

    def rot4(buf, i):
        return buf[:, (i % 4) * T:((i % 4) + 1) * T]

    def rot3(buf, i):
        return buf[:, (i % 3) * T:((i % 3) + 1) * T]

    mean_row = rows[0:1, 0:T]
    msq_row = rows[0:1, T:2 * T]
    var_row = rows[0:1, 2 * T:3 * T]
    rstd_row = rows[0:1, 3 * T:4 * T]
    t_mean = Tr(); t_msq = Tr(); t_var = Tr(); t_rstd = Tr()
    t_rb = Tr()

    def ring_view(s, K):
        return ring[:, s * H: s * H + K]

    ring_state = {"i": 0, "q": 0}

    def next_slot():
        s = ring_state["i"] % RING
        ring_state["i"] += 1
        return s

    cvst = {"q": 0, "d": 0, "st": 0}

    def fetch_weight(nm, j, c):
        nb, K = W_GEOM[nm]
        if c == 0:
            q = cvst["q"] % 3
            cvst["q"] += 1
            s = 6 + cvst["d"] % 3
            cvst["d"] += 1
            stage = ring_f32[:, q * H:q * H + K]
            rsv = ring_view(s, K)
            P.op("sync",
                 lambda e, o=stage, ap=w_in[nm].ap()[j * 128:(j + 1) * 128, :]:
                 e.dma_start(o, ap),
                 reads=[], writes=[t_ring[2 * q], t_ring[2 * q + 1]],
                 sem=f"cd{q}", inc=16)
            eng = ("vector", "gpsimd")[(j + ord(nm[1])) % 2]
            P.op(eng, lambda e, o=rsv, i=stage: e.tensor_copy(o, i),
                 reads=[t_ring[2 * q], t_ring[2 * q + 1]], writes=[t_ring[s]])
            P.op("gpsimd",
                 lambda e, o=w_scr[nm].ap()[j * 128:(j + 1) * 128, :], i=rsv:
                 e.dma_start(o, i),
                 reads=[t_ring[s]], writes=[t_scr[nm][j]], sem=f"sd{s}", inc=16)
        else:
            s = next_slot()
            rsv = ring_view(s, K)
            P.op("sync",
                 lambda e, o=rsv, ap=w_scr[nm].ap()[j * 128:(j + 1) * 128, :]:
                 e.dma_start(o, ap),
                 reads=[t_scr[nm][j]], writes=[t_ring[s]], sem=f"wd{s}", inc=16)
        return s

    def mm_group(ps_i, slot, K, rhs_views, rhs_trs, extra_reads=()):
        nk = K // 128
        rsv = ring_view(slot, K)
        for kh in range(nk):
            def fn(e, rsv=rsv, kh=kh, v=rhs_views[kh], p=psum[ps_i],
                   st=(kh == 0), sp=(kh == nk - 1)):
                return e.matmul(p[:, :], rsv[:, kh * 128:(kh + 1) * 128], v,
                                start=st, stop=sp)
            extra = list(extra_reads) if kh == 0 else []
            P.op("tensor", fn,
                 reads=[t_ring[slot], rhs_trs[kh]] + extra,
                 writes=[t_ps[ps_i]])

    P.op("sync", lambda e: e.dma_start(cv_sb[:], cvec_d.ap()[:, :]),
         reads=[], writes=[t_cv], sem="cd0", inc=16)
    for j in range(ND):
        P.op("sync",
             lambda e, o=xv(0, j), ap=x_d.ap()[j * 128:(j + 1) * 128, 0:T]:
             e.dma_start(o, ap),
             reads=[], writes=[t_x[0][j]], sem=f"xd{j}", inc=16)
    P.op("vector", lambda e: e.memset(ones_bf[:], 1.0), writes=[t_ones])
    P.op("vector", lambda e: e.memset(ones_r[:], 1.0), writes=[t_onesr])
    t_cb = Tr()
    P.op("vector", lambda e: e.memset(cb[:, 0:1], 1.000001), writes=[t_cb])
    P.op("vector", lambda e: e.memset(cb[:, 1:2], 1e-5), writes=[t_cb])
    P.op("vector", lambda e: e.memset(cb[:, 2:3], -0.6931471805599453), writes=[t_cb])
    P.op("vector", lambda e: e.memset(hc[:], 0.0), writes=[t_hc])
    P.op("vector", lambda e: e.memset(xn_bf[:], 0.0), writes=t_xn)
    P.op("vector", lambda e: e.memset(u_bf[:], 0.0), writes=t_u)
    lam_v = cv_sb[:, 60:72]
    P.op("scalar", lambda e: e.activation(s_half[:], lam_v, AF.Exp),
         reads=[t_cv], writes=[t_sh])
    P.op("scalar", lambda e: e.activation(s_half[:], s_half[:], AF.Ln, bias=1.0),
         reads=[], writes=[t_sh])
    P.op("scalar", lambda e: e.activation(s_half[:], s_half[:], AF.Copy, scale=-4.0),
         reads=[], writes=[t_sh])
    P.op("scalar", lambda e: e.activation(bi_h[:], cv_sb[:, 72:84], AF.Copy, scale=0.5),
         reads=[t_cv], writes=[t_bih])
    P.op("scalar", lambda e: e.activation(bg_h[:], cv_sb[:, 84:96], AF.Copy, scale=0.5),
         reads=[t_cv], writes=[t_bgh])

    xn_views = tuple(xnv(kh, 1, T + 1) for kh in range(ND))
    y_views = tuple(tile(y_bf, kh) for kh in range(NH))
    xr_views = tuple(tile(xr_bf, kh) for kh in range(ND))
    xk_views = tuple(tile(xk_bf, kh) for kh in range(ND))
    ka_views = tuple(tile(ka_bf, kh) for kh in range(ND))
    gh_views = tuple(tile(gg_bf, kh) for kh in range(NH))

    def plan_casts(cc, j):
        P.op("vector", lambda e, o=rot3(xbf4, j), i=xv(cc, j):
             e.tensor_copy(o, i),
             reads=[t_x[cc % 2][j]], writes=[t_xbf[j % 3]])
        P.op("scalar", lambda e, o=rot3(sq4, j), i=xv(cc, j):
             e.activation(o, i, AF.Square),
             reads=[t_x[cc % 2][j]], writes=[t_sq[j % 3]])

    def plan_A(cc):
        P.scope = f"A{cc}"
        """LN stats + normalize + ffn token-shift prep for chunk cc; planned
        before the previous chunk's D stage so it overlaps it."""
        tx = t_x[cc % 2]
        for j in range(ND):
            if j >= 3:
                plan_casts(cc, j)
            P.op("tensor", lambda e, r=rot3(xbf4, j), j=j:
                 e.matmul(psum[0][0:1, :], ones_bf[:, :], r,
                          start=(j == 0), stop=(j == ND - 1)),
                 reads=[t_ones, t_xbf[j % 3]], writes=[t_ps[0]])
            P.op("tensor", lambda e, r=rot3(sq4, j), j=j:
                 e.matmul(psum[1][0:1, :], ones_bf[:, :], r,
                          start=(j == 0), stop=(j == ND - 1)),
                 reads=[t_ones, t_sq[j % 3]], writes=[t_ps[1]])
        P.op("scalar", lambda e: e.activation(mean_row, psum[0][0:1, :],
                                              AF.Copy, scale=1.0 / D),
             reads=[t_ps[0]], writes=[t_mean])
        P.op("scalar", lambda e: e.activation(msq_row, psum[0][0:1, :],
                                              AF.Square, scale=1.0 / D),
             reads=[t_ps[0]], writes=[t_msq])
        P.op("scalar", lambda e: e.activation(var_row, psum[1][0:1, :],
                                              AF.Copy, scale=1.0 / D),
             reads=[t_ps[1]], writes=[t_var])
        P.op("vector", lambda e: e.tensor_sub(var_row, var_row, msq_row),
             reads=[t_msq], writes=[t_var])
        P.op("scalar", lambda e: e.activation(var_row, var_row, AF.Ln,
                                              bias=cb[0:1, 1:2]),
             reads=[t_cb], writes=[t_var])
        P.op("scalar", lambda e: e.activation(rstd_row, var_row, AF.Exp,
                                              scale=-0.5),
             reads=[t_var], writes=[t_rstd])
        P.op("tensor", lambda e: e.matmul(psum[0][:, :], ones_r[:, :],
                                          mean_row, start=True, stop=True),
             reads=[t_onesr, t_mean], writes=[t_ps[0]])
        P.op("tensor", lambda e: e.matmul(psum[1][:, :], ones_r[:, :],
                                          rstd_row, start=True, stop=True),
             reads=[t_onesr, t_rstd], writes=[t_ps[1]])
        P.op("scalar", lambda e: e.activation(rb_sb[:], psum[1][:, :],
                                              AF.Copy),
             reads=[t_ps[1]], writes=[t_rb])
        for j in range(ND):
            P.op("vector", lambda e, o=rot(diff2, j), i=xv(cc, j):
                 e.tensor_sub(o, i, psum[0][:, :]),
                 reads=[tx[j], t_ps[0]], writes=[t_diff[j % 2]])
            if j % 2 == 0:
                P.op("vector", lambda e, o=xnv(j, 1, T + 1), i=rot(diff2, j):
                     e.tensor_mul(o, i, psum[1][:, :]),
                     reads=[t_diff[j % 2], t_ps[1]], writes=[t_xn[j]])
            else:
                P.op("gpsimd", lambda e, o=xnv(j, 1, T + 1), i=rot(diff2, j):
                     e.tensor_mul(o, i, rb_sb[:]),
                     reads=[t_diff[j % 2], t_rb], writes=[t_xn[j]])
        for j in range(ND):
            P.op("vector", lambda e, o=rot(diff2, j), a=xnv(j, 1, T + 1),
                 b=xnv(j, 0, T): e.tensor_sub(o, a, b),
                 reads=[t_xn[j]], writes=[t_diff[j % 2]])
            P.op("vector", lambda e, o=tile(xr_bf, j), d=rot(diff2, j),
                 m=cv_sb[:, 96 + j:97 + j], p=xnv(j, 0, T):
                 e.scalar_tensor_tensor(o, d, m, p, ALU.mult, ALU.add),
                 reads=[t_diff[j % 2], t_cv, t_xn[j]], writes=[t_xr[j]])
            P.op("vector", lambda e, o=tile(xk_bf, j), d=rot(diff2, j),
                 m=cv_sb[:, 104 + j:105 + j], p=xnv(j, 0, T):
                 e.scalar_tensor_tensor(o, d, m, p, ALU.mult, ALU.add),
                 reads=[t_diff[j % 2], t_cv, t_xn[j]], writes=[t_xk[j]])

    def plan_B(c):
        P.scope = f"Bu{c}"
        for j in range(NH):
            pu = (2, 3, 0, 1)[j % 4]
            s = fetch_weight("wi", j, c)
            mm_group(pu, s, D, xn_views, t_xn)
            P.op("scalar", lambda e, o=uv(j, 3, T + 3), p=psum[pu]:
                 e.activation(o, p[:, :], AF.Copy),
                 reads=[t_ps[pu]], writes=[t_u[j]])
        P.scope = f"conv{c}"
        for j in range(NH):
            P.op("vector", lambda e, o=rot(scr2, j), i=uv(j, 0, T),
                 w=cv_sb[:, 4 * j:4 * j + 1], b=cv_sb[:, 48 + j:49 + j]:
                 e.tensor_scalar(o, i, w, b, ALU.mult, ALU.add),
                 reads=[t_u[j], t_cv], writes=[t_s2[j % 2]])
            for tap in (1, 2):
                P.op("vector", lambda e, o=rot(scr2, j), i=uv(j, tap, T + tap),
                     w=cv_sb[:, 4 * j + tap:4 * j + tap + 1], s2=rot(scr2, j):
                     e.scalar_tensor_tensor(o, i, w, s2, ALU.mult, ALU.add),
                     reads=[t_u[j], t_cv], writes=[t_s2[j % 2]])
            P.op("vector", lambda e, o=tile(y_bf, j), i=uv(j, 3, T + 3),
                 w=cv_sb[:, 4 * j + 3:4 * j + 4], s2=rot(scr2, j):
                 e.scalar_tensor_tensor(o, i, w, s2, ALU.mult, ALU.add),
                 reads=[t_u[j], t_cv, t_s2[j % 2]], writes=[t_y[j]])
            P.op("gpsimd", lambda e, o=uv(j, 0, 3), i=uv(j, T, T + 3):
                 e.tensor_copy(o, i),
                 reads=[], writes=[t_u[j]])
        P.scope = f"Bwr{c}"
        for j in range(ND):
            s = fetch_weight("wr", j, c)
            mm_group(4 + j % 2, s, D, xr_views, t_xr)
            P.op("scalar", lambda e, o=tile(tf_hf, j), p=psum[4 + j % 2]:
                 e.activation(o, p[:, :], AF.Tanh, scale=0.5),
                 reads=[t_ps[4 + j % 2]], writes=[t_tf[j]])
        P.scope = f"Bg{c}"
        for j in range(NH):
            s = fetch_weight("wg", j, c)
            mm_group(6 + j % 2, s, D, xn_views, t_xn)
            P.op("scalar", lambda e, o=tile(gg_bf, j), p=psum[6 + j % 2]:
                 e.activation(o, p[:, :], AF.Gelu),
                 reads=[t_ps[6 + j % 2]], writes=[t_gg[j]])
        for j in range(ND):
            P.op("gpsimd", lambda e, o=xnv(j, 0, 1), i=xnv(j, T, T + 1):
                 e.tensor_copy(o, i),
                 reads=[], writes=[t_xn[j]])

    def plan_CE(c, mid=None):
        def plan_c1(j):
            P.scope = f"C1_{c}_{j}"
            pi = (4, 6, 2)[j % 3]
            pr = (5, 7, 3)[j % 3]
            s = fetch_weight("lwi", j, c)
            mm_group(pi, s, H, y_views, t_y)
            P.op("scalar", lambda e, o=tile(ti_hf, j), p=psum[pi],
                 b=bi_h[:, j:j + 1]:
                 e.activation(o, p[:, :], AF.Tanh, bias=b, scale=0.5),
                 reads=[t_ps[pi], t_bih], writes=[t_ti[j]])
            s = fetch_weight("lwg", j, c)
            mm_group(pr, s, H, y_views, t_y)
            P.op("scalar", lambda e, o=tile(trg_hf, j), p=psum[pr],
                 b=bg_h[:, j:j + 1]:
                 e.activation(o, p[:, :], AF.Tanh, bias=b, scale=0.5),
                 reads=[t_ps[pr], t_bgh], writes=[t_trg[j]])

        def plan_c2(j):
            P.scope = f"C2_{c}_{j}"
            sh = s_half[:, j:j + 1]
            P.op("scalar", lambda e, o=rot3(alpha2, j), i=tile(trg_hf, j),
                 sh=sh: e.activation(o, i, AF.Exp, bias=sh, scale=sh),
                 reads=[t_trg[j], t_sh], writes=[t_alpha[j % 3]])
            P.op("scalar", lambda e, o=rot(beta2, j), i=rot3(alpha2, j):
                 e.activation(o, i, AF.Square),
                 reads=[t_alpha[j % 3]], writes=[t_beta[j % 2]])
            P.op("scalar", lambda e, o=rot(beta2, j), i=rot(beta2, j):
                 e.activation(o, i, AF.Ln, bias=cb[:, 0:1], scale=-1.0),
                 reads=[t_cb], writes=[t_beta[j % 2]])
            P.op("scalar", lambda e, o=rot(beta2, j), i=rot(beta2, j):
                 e.activation(o, i, AF.Exp, bias=cb[:, 2:3], scale=0.5),
                 reads=[t_cb], writes=[t_beta[j % 2]])
            P.op("vector", lambda e, o=rot3(igs2, j), i=tile(ti_hf, j),
                 b=rot(beta2, j):
                 e.scalar_tensor_tensor(o, i, 1.0, b, ALU.add, ALU.mult),
                 reads=[t_ti[j], t_beta[j % 2]], writes=[t_igs[j % 3]])
            P.op("gpsimd", lambda e, o=rot(xbh2, j), i=rot3(igs2, j),
                 y=tile(y_bf, j): e.tensor_mul(o, i, y),
                 reads=[t_igs[j % 3], t_y[j]], writes=[t_xbh[j % 2]])
            P.op("vector", lambda e, o=rot3(igs2, j), a=rot3(alpha2, j),
                 b=rot(xbh2, j), i=hc[:, j:j + 1]:
                 e.tensor_tensor_scan(o, a, b, i, ALU.mult, ALU.add),
                 reads=[t_alpha[j % 3], t_xbh[j % 2], t_hc],
                 writes=[t_igs[j % 3]])
            P.op("gpsimd", lambda e, o=hc[:, j:j + 1],
                 i=rot3(igs2, j)[:, T - 1:T]: e.tensor_copy(o, i),
                 reads=[t_igs[j % 3]], writes=[t_hc])
            P.op("gpsimd", lambda e, o=tile(gg_bf, j), h=rot3(igs2, j):
                 e.tensor_mul(o, o, h),
                 reads=[t_igs[j % 3]], writes=[t_gg[j]])

        def plan_e1(j):
            P.scope = f"E1_{c}_{j}"
            s = fetch_weight("wk", j, c)
            mm_group(0 + j % 2, s, D, xk_views, t_xk)
            P.op("scalar", lambda e, o=rot(scr2, j), p=psum[0 + j % 2]:
                 e.activation(o, p[:, :], AF.Relu),
                 reads=[t_ps[0 + j % 2]], writes=[t_s2[j % 2]])
            P.op("scalar", lambda e, o=tile(ka_bf, j), i=rot(scr2, j):
                 e.activation(o, i, AF.Square, scale=0.7071067811865476),
                 reads=[t_s2[j % 2]], writes=[t_ka[j]])

        def plan_e2(j):
            P.scope = f"E2_{c}_{j}"
            s = fetch_weight("wv", j, c)
            mm_group(2 + j % 2, s, D, ka_views, t_ka)
            P.op("vector", lambda e, o=rot(scr2, j), i=tile(tf_hf, j),
                 p=psum[2 + j % 2]:
                 e.scalar_tensor_tensor(o, i, 1.0, p[:, :], ALU.add,
                                        ALU.mult),
                 reads=[t_tf[j], t_ps[2 + j % 2]], writes=[t_s2[j % 2]])
            P.op("vector", lambda e, o=xv(c, j), i=rot(scr2, j):
                 e.tensor_add(o, o, i),
                 reads=[t_s2[j % 2]], writes=[t_x[c % 2][j]])

        def weave(aa, bb):
            out = []
            na, nb = len(aa), len(bb)
            i = k = 0
            while i < na or k < nb:
                if i < na:
                    out.append(aa[i]); i += 1
                if k < nb:
                    out.append(bb[k]); k += 1
            return out

        for j in range(6):
            plan_c1(j)
        for unit in weave([(plan_e1, j) for j in range(ND)],
                          [(plan_c2, j) for j in range(6)]):
            unit[0](unit[1])
        if mid is not None:
            mid()
        for j in range(6, NH):
            plan_c1(j)
        for unit in weave([(plan_e2, j) for j in range(ND)],
                          [(plan_c2, j) for j in range(6, NH)]):
            unit[0](unit[1])

    def plan_D(c):
        P.scope = f"D{c}"
        col0 = c * T
        for j in range(ND):
            s = fetch_weight("wo", j, c)
            mm_group(4 + j % 4, s, H, gh_views, t_gg)
            P.op("vector", lambda e, o=xv(c, j), p=psum[4 + j % 4]:
                 e.tensor_add(o, o, p[:, :]),
                 reads=[t_ps[4 + j % 4]], writes=[t_x[c % 2][j]])
            P.op("scalar",
                 lambda e, i=xv(c, j), ap=out_d.ap()[j * 128:(j + 1) * 128,
                                                     col0:col0 + T]:
                 e.dma_start(ap, i),
                 reads=[t_x[c % 2][j]], writes=[], sem=f"od{j}", inc=16)

    for j in range(3):
        plan_casts(0, j)
    plan_A(0)

    for c in range(NCH):
        if c + 1 < NCH:
            for j in range(ND):
                P.op("scalar",
                     lambda e, o=xv(c + 1, j),
                     ap=x_d.ap()[j * 128:(j + 1) * 128,
                                 (c + 1) * T:(c + 2) * T]:
                     e.dma_start(o, ap),
                     reads=[], writes=[t_x[(c + 1) % 2][j]],
                     sem=f"xd{j}", inc=16)
        plan_B(c)

        def mid():
            for j in range(3):
                plan_casts(c + 1, j)
            plan_A(c + 1)
        plan_CE(c, mid if c + 1 < NCH else None)
        plan_D(c)

    for j in range(ND):
        P.wait("sync", f"od{j}", 16 * NCH)

    with nc.Block() as block:
        P.emit(block, sems)

    return nc


def _blk(W, nb, K):
    a = np.asarray(W, dtype=np.float32).reshape(nb, 128, K // 128, 128)
    a = a.transpose(0, 3, 2, 1)
    return np.ascontiguousarray(a.reshape(nb * 128, K))


def make_inputs(x_b, Wg, Wi, conv_w, conv_b, lru_Wi, lru_bi, lru_Wg, lru_bg,
                lam, Wo, W_r, W_k, W_v, mu_r, mu_k):
    cv = np.zeros((128, 112), np.float32)
    cv[:, 0:48] = np.asarray(conv_w, np.float32).reshape(NH, 128, 4) \
        .transpose(1, 0, 2).reshape(128, 48)
    cv[:, 48:60] = np.asarray(conv_b, np.float32).reshape(NH, 128).T
    cv[:, 60:72] = np.asarray(lam, np.float32).reshape(NH, 128).T
    cv[:, 72:84] = np.asarray(lru_bi, np.float32).reshape(NH, 128).T
    cv[:, 84:96] = np.asarray(lru_bg, np.float32).reshape(NH, 128).T
    cv[:, 96:104] = np.asarray(mu_r, np.float32).reshape(ND, 128).T
    cv[:, 104:112] = np.asarray(mu_k, np.float32).reshape(ND, 128).T
    shared = {
        "cvec": cv,
        "wg": _blk(Wg, NH, D),
        "wi": _blk(Wi, NH, D),
        "lwi": _blk(lru_Wi, NH, H),
        "lwg": _blk(lru_Wg, NH, H),
        "wo": _blk(Wo, ND, H),
        "wr": _blk(W_r, ND, D),
        "wk": _blk(W_k, ND, D),
        "wv": _blk(W_v, ND, D),
    }
    m = dict(shared)
    m["x"] = np.ascontiguousarray(np.asarray(x_b, np.float32).T)
    return m


def kernel(L0_attn_x, L0_attn_Wg, L0_attn_Wi, L0_attn_conv_w, L0_attn_conv_b,
           L0_attn_lru_Wi, L0_attn_lru_bi, L0_attn_lru_Wg, L0_attn_lru_bg,
           L0_attn_lam, L0_attn_Wo, L0_ffn_W_r, L0_ffn_W_k, L0_ffn_W_v,
           L0_ffn_mu_r, L0_ffn_mu_k):
    x = np.asarray(L0_attn_x, np.float32)
    B, S, _ = x.shape
    nc = build_program(S, 512)
    in_maps = []
    for b in range(B):
        in_maps.append(make_inputs(
            x[b], L0_attn_Wg, L0_attn_Wi, L0_attn_conv_w, L0_attn_conv_b,
            L0_attn_lru_Wi, L0_attn_lru_bi, L0_attn_lru_Wg, L0_attn_lru_bg,
            L0_attn_lam, L0_attn_Wo, L0_ffn_W_r, L0_ffn_W_k, L0_ffn_W_v,
            L0_ffn_mu_r, L0_ffn_mu_k))
    res = run_bass_kernel_spmd(nc, in_maps, core_ids=list(range(N_CORES)))
    out = np.stack([np.ascontiguousarray(np.asarray(r["out"]).T)
                    for r in res.results], axis=0)
    return out.astype(np.float32)


# version 32
# speedup vs baseline: 1.0325x; 1.0272x over previous
import numpy as np
import concourse.bass as bass
import concourse.mybir as mybir
from concourse.bass_utils import run_bass_kernel_spmd

F32 = mybir.dt.float32
BF16 = mybir.dt.bfloat16
F16 = mybir.dt.float16
AF = mybir.ActivationFunctionType
ALU = mybir.AluOpType

D, H = 1024, 1536
ND, NH = 8, 12
N_CORES = 8

ENGINES = ("tensor", "scalar", "vector", "gpsimd", "sync")
_LAST_PLOG = {}


class Tr:
    __slots__ = ("w", "r", "psum")

    def __init__(self, psum=False):
        self.w = None
        self.r = []
        self.psum = psum


class Prog:

    def __init__(self):
        self.q = {e: [] for e in ENGINES}
        self.ctr = {}
        self.synced = {e: {} for e in ENGINES}
        self.scope = ""
        self.log = {}

    def _deps(self, reads, writes):
        deps = {}

        def add(tok):
            if tok is None:
                return
            s, v = tok
            if deps.get(s, 0) < v:
                deps[s] = v

        for t in reads:
            add(t.w)
        for t in writes:
            add(t.w)
            for tok in t.r:
                add(tok)
        return deps

    def op(self, eng, fn, reads=(), writes=(), sem=None, inc=1):
        deps = self._deps(reads, writes)
        if eng in ("scalar", "vector"):
            other = "vector" if eng == "scalar" else "scalar"
            for t in reads:
                if getattr(t, "psum", False):
                    for (s0, v0) in t.r:
                        if s0 == other and deps.get(s0, 0) < v0:
                            deps[s0] = v0
        if inc == 16 and sem is not None:
            prev = self.ctr.get(sem, 0)
            if prev and deps.get(sem, 0) < prev:
                deps[sem] = prev
        waits = []
        for s in sorted(deps):
            v = deps[s]
            if eng == "tensor" and s == "tensor":
                continue
            if self.synced[eng].get(s, 0) < v:
                waits.append((s, v))
                self.synced[eng][s] = v
        s = sem if sem is not None else eng
        self.ctr[s] = self.ctr.get(s, 0) + inc
        tok = (s, self.ctr[s])
        self.log[tok] = self.scope
        self.q[eng].append((waits, fn, s, inc))
        for t in reads:
            t.r.append(tok)
        for t in writes:
            t.w = tok
            t.r = []
        return tok

    def wait(self, eng, sem, val):
        self.q[eng].append(([(sem, val)], None, None, 0))
        self.synced[eng][sem] = max(self.synced[eng].get(sem, 0), val)

    def emit(self, block, sems):
        for eng in ENGINES:
            entries = self.q[eng]

            def make_body(entries=entries):
                def body(e):
                    for waits, fn, s, inc in entries:
                        for ws, wv in waits:
                            e.wait_ge(sems[ws], wv)
                        if fn is not None:
                            ins = fn(e)
                            ins.then_inc(sems[s], inc)
                return body

            getattr(block, eng)(make_body())


def build_program(S, T):
    NCH = S // T
    assert S % T == 0
    nc = bass.Bass("TRN2", target_bir_lowering=False)

    x_d = nc.dram_tensor("x", [D, S], F32, kind="ExternalInput")
    out_d = nc.dram_tensor("out", [D, S], F32, kind="ExternalOutput")
    cvec_d = nc.dram_tensor("cvec", [128, 112], F32, kind="ExternalInput")
    w_in = {}
    w_scr = {}
    W_GEOM = {
        "wi": (NH, D), "wr": (ND, D), "wg": (NH, D),
        "lwi": (NH, H), "lwg": (NH, H),
        "wk": (ND, D), "wv": (ND, D), "wo": (ND, H),
    }
    for nm, (nb, K) in W_GEOM.items():
        w_in[nm] = nc.dram_tensor(nm, [nb * 128, K], F32, kind="ExternalInput")
        w_scr[nm] = nc.dram_tensor(nm + "_s", [nb * 128, K], BF16, kind="Internal")

    from contextlib import ExitStack
    stack = ExitStack()

    def sb(name, shape, dt):
        return stack.enter_context(nc.sbuf_tensor(name, shape, dt))

    x_sb = sb("x_sb", [128, 2 * ND * T], F32)
    xn_bf = sb("xn_bf", [128, ND * (T + 1)], BF16)
    u_bf = sb("u_bf", [128, NH * (T + 3)], BF16)
    y_bf = sb("y_bf", [128, NH * T], BF16)
    gg_bf = sb("gg_bf", [128, NH * T], BF16)
    ti_hf = sb("ti_hf", [128, NH * T], F16)
    trg_hf = sb("trg_hf", [128, NH * T], F16)
    tf_hf = sb("tf_hf", [128, ND * T], F16)
    xr_bf = sb("xr_bf", [128, ND * T], BF16)
    xk_bf = sb("xk_bf", [128, ND * T], BF16)
    ka_bf = sb("ka_bf", [128, ND * T], BF16)
    alpha2 = sb("alpha2", [128, 3 * T], F32)
    xbh2 = sb("xbh2", [128, 2 * T], F32)
    beta2 = sb("beta2", [128, 2 * T], F32)
    igs2 = sb("igs2", [128, 3 * T], F32)
    scr2 = sb("scr2", [128, 2 * T], F32)
    diff2 = sb("diff2", [128, 2 * T], F32)
    xbf4 = sb("xbf4", [128, 3 * T], BF16)
    sq4 = sb("sq4", [128, 3 * T], BF16)
    rows = sb("rows", [1, 4 * T], F32)
    cv_sb = sb("cv_sb", [128, 112], F32)
    s_half = sb("s_half", [128, NH], F32)
    bi_h = sb("bi_h", [128, NH], F32)
    bg_h = sb("bg_h", [128, NH], F32)
    hc = sb("hc", [128, NH], F32)
    ones_bf = sb("ones_bf", [128, 1], BF16)
    ones_r = sb("ones_r", [1, 128], F32)
    cb = sb("cb", [128, 3], F32)
    rb_sb = sb("rb_sb", [128, T], F32)
    RING = 9
    ring = sb("ring", [128, RING * H], BF16)
    ring_f32 = ring.bitcast(F32)

    psum = [stack.enter_context(nc.psum_tensor(f"pb{i}", [128, T], F32))
            for i in range(8)]

    sem_names = (["tensor", "scalar", "vector", "gpsimd", "cd0", "cd1", "cd2", "cd3"]
                 + [f"wd{s}" for s in range(RING)] + [f"sd{s}" for s in range(RING)]
                 + [f"xd{j}" for j in range(ND)] + [f"od{j}" for j in range(ND)])
    sems = {n: stack.enter_context(nc.semaphore("sem_" + n)) for n in sem_names}

    P = Prog()
    global _LAST_PLOG
    _LAST_PLOG = P.log

    t_x = [[Tr() for _ in range(ND)], [Tr() for _ in range(ND)]]
    t_xn = [Tr() for _ in range(ND)]
    t_u = [Tr() for _ in range(NH)]
    t_y = [Tr() for _ in range(NH)]
    t_gg = [Tr() for _ in range(NH)]
    t_ti = [Tr() for _ in range(NH)]
    t_trg = [Tr() for _ in range(NH)]
    t_tf = [Tr() for _ in range(ND)]
    t_xr = [Tr() for _ in range(ND)]
    t_xk = [Tr() for _ in range(ND)]
    t_ka = [Tr() for _ in range(ND)]
    t_alpha = [Tr() for _ in range(3)]
    t_xbh = [Tr() for _ in range(2)]
    t_beta = [Tr() for _ in range(2)]
    t_igs = [Tr() for _ in range(3)]
    t_s2 = [Tr() for _ in range(2)]
    t_diff = [Tr() for _ in range(2)]
    t_xbf = [Tr() for _ in range(3)]
    t_sq = [Tr() for _ in range(3)]
    t_rows = Tr()
    t_cv = Tr()
    t_sh = Tr()
    t_bih = Tr()
    t_bgh = Tr()
    t_hc = Tr()
    t_ones = Tr()
    t_onesr = Tr()
    t_ring = [Tr() for _ in range(RING)]
    t_cvt = Tr()
    t_ps = [Tr(psum=True) for _ in range(8)]
    t_scr = {nm: [Tr() for _ in range(nb)] for nm, (nb, K) in W_GEOM.items()}

    def xv(cc, j):
        b = ((cc % 2) * ND + j) * T
        return x_sb[:, b:b + T]

    def xnv(j, lo, hi):
        b = j * (T + 1)
        return xn_bf[:, b + lo: b + hi]

    def uv(j, lo, hi):
        b = j * (T + 3)
        return u_bf[:, b + lo: b + hi]

    def tile(t_buf, j):
        return t_buf[:, j * T:(j + 1) * T]

    def rot(buf, i):
        return buf[:, (i % 2) * T:((i % 2) + 1) * T]

    def rot4(buf, i):
        return buf[:, (i % 4) * T:((i % 4) + 1) * T]

    def rot3(buf, i):
        return buf[:, (i % 3) * T:((i % 3) + 1) * T]

    mean_row = rows[0:1, 0:T]
    msq_row = rows[0:1, T:2 * T]
    var_row = rows[0:1, 2 * T:3 * T]
    rstd_row = rows[0:1, 3 * T:4 * T]
    t_mean = Tr(); t_msq = Tr(); t_var = Tr(); t_rstd = Tr()
    t_rb = Tr()

    def ring_view(s, K):
        return ring[:, s * H: s * H + K]

    ring_state = {"i": 0, "q": 0}

    def next_slot():
        s = ring_state["i"] % RING
        ring_state["i"] += 1
        return s

    cvst = {"q": 0, "d": 0, "st": 0}

    def fetch_weight(nm, j, c):
        nb, K = W_GEOM[nm]
        if c == 0:
            q = cvst["q"] % 3
            cvst["q"] += 1
            s = 6 + cvst["d"] % 3
            cvst["d"] += 1
            stage = ring_f32[:, q * H:q * H + K]
            rsv = ring_view(s, K)
            P.op("sync",
                 lambda e, o=stage, ap=w_in[nm].ap()[j * 128:(j + 1) * 128, :]:
                 e.dma_start(o, ap),
                 reads=[], writes=[t_ring[2 * q], t_ring[2 * q + 1]],
                 sem=f"cd{q}", inc=16)
            eng = ("vector", "gpsimd")[(j + ord(nm[1])) % 2]
            P.op(eng, lambda e, o=rsv, i=stage: e.tensor_copy(o, i),
                 reads=[t_ring[2 * q], t_ring[2 * q + 1]], writes=[t_ring[s]])
            P.op("gpsimd",
                 lambda e, o=w_scr[nm].ap()[j * 128:(j + 1) * 128, :], i=rsv:
                 e.dma_start(o, i),
                 reads=[t_ring[s]], writes=[t_scr[nm][j]], sem=f"sd{s}", inc=16)
        else:
            s = next_slot()
            rsv = ring_view(s, K)
            P.op("sync",
                 lambda e, o=rsv, ap=w_scr[nm].ap()[j * 128:(j + 1) * 128, :]:
                 e.dma_start(o, ap),
                 reads=[t_scr[nm][j]], writes=[t_ring[s]], sem=f"wd{s}", inc=16)
        return s

    def mm_group(ps_i, slot, K, rhs_views, rhs_trs, extra_reads=()):
        nk = K // 128
        rsv = ring_view(slot, K)
        for kh in range(nk):
            def fn(e, rsv=rsv, kh=kh, v=rhs_views[kh], p=psum[ps_i],
                   st=(kh == 0), sp=(kh == nk - 1)):
                return e.matmul(p[:, :], rsv[:, kh * 128:(kh + 1) * 128], v,
                                start=st, stop=sp)
            extra = list(extra_reads) if kh == 0 else []
            P.op("tensor", fn,
                 reads=[t_ring[slot], rhs_trs[kh]] + extra,
                 writes=[t_ps[ps_i]])

    P.op("sync", lambda e: e.dma_start(cv_sb[:], cvec_d.ap()[:, :]),
         reads=[], writes=[t_cv], sem="cd0", inc=16)
    for j in range(ND):
        P.op("sync",
             lambda e, o=xv(0, j), ap=x_d.ap()[j * 128:(j + 1) * 128, 0:T]:
             e.dma_start(o, ap),
             reads=[], writes=[t_x[0][j]], sem=f"xd{j}", inc=16)
    P.op("vector", lambda e: e.memset(ones_bf[:], 1.0), writes=[t_ones])
    P.op("vector", lambda e: e.memset(ones_r[:], 1.0), writes=[t_onesr])
    t_cb = Tr()
    P.op("vector", lambda e: e.memset(cb[:, 0:1], 1.000001), writes=[t_cb])
    P.op("vector", lambda e: e.memset(cb[:, 1:2], 1e-5), writes=[t_cb])
    P.op("vector", lambda e: e.memset(cb[:, 2:3], -0.6931471805599453), writes=[t_cb])
    P.op("vector", lambda e: e.memset(hc[:], 0.0), writes=[t_hc])
    P.op("vector", lambda e: e.memset(xn_bf[:], 0.0), writes=t_xn)
    P.op("vector", lambda e: e.memset(u_bf[:], 0.0), writes=t_u)
    lam_v = cv_sb[:, 60:72]
    P.op("scalar", lambda e: e.activation(s_half[:], lam_v, AF.Exp),
         reads=[t_cv], writes=[t_sh])
    P.op("scalar", lambda e: e.activation(s_half[:], s_half[:], AF.Ln, bias=1.0),
         reads=[], writes=[t_sh])
    P.op("scalar", lambda e: e.activation(s_half[:], s_half[:], AF.Copy, scale=-4.0),
         reads=[], writes=[t_sh])
    P.op("scalar", lambda e: e.activation(bi_h[:], cv_sb[:, 72:84], AF.Copy, scale=0.5),
         reads=[t_cv], writes=[t_bih])
    P.op("scalar", lambda e: e.activation(bg_h[:], cv_sb[:, 84:96], AF.Copy, scale=0.5),
         reads=[t_cv], writes=[t_bgh])

    xn_views = tuple(xnv(kh, 1, T + 1) for kh in range(ND))
    y_views = tuple(tile(y_bf, kh) for kh in range(NH))
    xr_views = tuple(tile(xr_bf, kh) for kh in range(ND))
    xk_views = tuple(tile(xk_bf, kh) for kh in range(ND))
    ka_views = tuple(tile(ka_bf, kh) for kh in range(ND))
    gh_views = tuple(tile(gg_bf, kh) for kh in range(NH))

    def plan_casts(cc, j):
        P.op("vector", lambda e, o=rot3(xbf4, j), i=xv(cc, j):
             e.tensor_copy(o, i),
             reads=[t_x[cc % 2][j]], writes=[t_xbf[j % 3]])
        P.op("scalar", lambda e, o=rot3(sq4, j), i=xv(cc, j):
             e.activation(o, i, AF.Square),
             reads=[t_x[cc % 2][j]], writes=[t_sq[j % 3]])

    def plan_A(cc):
        P.scope = f"A{cc}"
        """LN stats + normalize + ffn token-shift prep for chunk cc; planned
        before the previous chunk's D stage so it overlaps it."""
        tx = t_x[cc % 2]
        for j in range(ND):
            if j >= 3:
                plan_casts(cc, j)
            P.op("tensor", lambda e, r=rot3(xbf4, j), j=j:
                 e.matmul(psum[0][0:1, :], ones_bf[:, :], r,
                          start=(j == 0), stop=(j == ND - 1)),
                 reads=[t_ones, t_xbf[j % 3]], writes=[t_ps[0]])
            P.op("tensor", lambda e, r=rot3(sq4, j), j=j:
                 e.matmul(psum[1][0:1, :], ones_bf[:, :], r,
                          start=(j == 0), stop=(j == ND - 1)),
                 reads=[t_ones, t_sq[j % 3]], writes=[t_ps[1]])
        P.op("scalar", lambda e: e.activation(mean_row, psum[0][0:1, :],
                                              AF.Copy, scale=1.0 / D),
             reads=[t_ps[0]], writes=[t_mean])
        P.op("scalar", lambda e: e.activation(msq_row, psum[0][0:1, :],
                                              AF.Square, scale=1.0 / D),
             reads=[t_ps[0]], writes=[t_msq])
        P.op("scalar", lambda e: e.activation(var_row, psum[1][0:1, :],
                                              AF.Copy, scale=1.0 / D),
             reads=[t_ps[1]], writes=[t_var])
        P.op("vector", lambda e: e.tensor_sub(var_row, var_row, msq_row),
             reads=[t_msq], writes=[t_var])
        P.op("scalar", lambda e: e.activation(var_row, var_row, AF.Ln,
                                              bias=cb[0:1, 1:2]),
             reads=[t_cb], writes=[t_var])
        P.op("scalar", lambda e: e.activation(rstd_row, var_row, AF.Exp,
                                              scale=-0.5),
             reads=[t_var], writes=[t_rstd])
        P.op("tensor", lambda e: e.matmul(psum[0][:, :], ones_r[:, :],
                                          mean_row, start=True, stop=True),
             reads=[t_onesr, t_mean], writes=[t_ps[0]])
        P.op("tensor", lambda e: e.matmul(psum[1][:, :], ones_r[:, :],
                                          rstd_row, start=True, stop=True),
             reads=[t_onesr, t_rstd], writes=[t_ps[1]])
        P.op("scalar", lambda e: e.activation(rb_sb[:], psum[1][:, :],
                                              AF.Copy),
             reads=[t_ps[1]], writes=[t_rb])
        for j in range(ND):
            P.op("vector", lambda e, o=rot(diff2, j), i=xv(cc, j):
                 e.tensor_sub(o, i, psum[0][:, :]),
                 reads=[tx[j], t_ps[0]], writes=[t_diff[j % 2]])
            if j % 2 == 0:
                P.op("vector", lambda e, o=xnv(j, 1, T + 1), i=rot(diff2, j):
                     e.tensor_mul(o, i, psum[1][:, :]),
                     reads=[t_diff[j % 2], t_ps[1]], writes=[t_xn[j]])
            else:
                P.op("vector" if cc == 0 else "gpsimd",
                     lambda e, o=xnv(j, 1, T + 1), i=rot(diff2, j):
                     e.tensor_mul(o, i, rb_sb[:]),
                     reads=[t_diff[j % 2], t_rb], writes=[t_xn[j]])
        for j in range(ND):
            P.op("vector", lambda e, o=rot(diff2, j), a=xnv(j, 1, T + 1),
                 b=xnv(j, 0, T): e.tensor_sub(o, a, b),
                 reads=[t_xn[j]], writes=[t_diff[j % 2]])
            P.op("vector", lambda e, o=tile(xr_bf, j), d=rot(diff2, j),
                 m=cv_sb[:, 96 + j:97 + j], p=xnv(j, 0, T):
                 e.scalar_tensor_tensor(o, d, m, p, ALU.mult, ALU.add),
                 reads=[t_diff[j % 2], t_cv, t_xn[j]], writes=[t_xr[j]])
            P.op("vector", lambda e, o=tile(xk_bf, j), d=rot(diff2, j),
                 m=cv_sb[:, 104 + j:105 + j], p=xnv(j, 0, T):
                 e.scalar_tensor_tensor(o, d, m, p, ALU.mult, ALU.add),
                 reads=[t_diff[j % 2], t_cv, t_xn[j]], writes=[t_xk[j]])

    def plan_B(c):
        P.scope = f"Bu{c}"
        for j in range(NH):
            pu = (2, 3, 0, 1)[j % 4]
            s = fetch_weight("wi", j, c)
            mm_group(pu, s, D, xn_views, t_xn)
            P.op("scalar", lambda e, o=uv(j, 3, T + 3), p=psum[pu]:
                 e.activation(o, p[:, :], AF.Copy),
                 reads=[t_ps[pu]], writes=[t_u[j]])
        P.scope = f"conv{c}"
        for j in range(NH):
            P.op("vector", lambda e, o=rot(scr2, j), i=uv(j, 0, T),
                 w=cv_sb[:, 4 * j:4 * j + 1], b=cv_sb[:, 48 + j:49 + j]:
                 e.tensor_scalar(o, i, w, b, ALU.mult, ALU.add),
                 reads=[t_u[j], t_cv], writes=[t_s2[j % 2]])
            for tap in (1, 2):
                P.op("vector", lambda e, o=rot(scr2, j), i=uv(j, tap, T + tap),
                     w=cv_sb[:, 4 * j + tap:4 * j + tap + 1], s2=rot(scr2, j):
                     e.scalar_tensor_tensor(o, i, w, s2, ALU.mult, ALU.add),
                     reads=[t_u[j], t_cv], writes=[t_s2[j % 2]])
            P.op("vector", lambda e, o=tile(y_bf, j), i=uv(j, 3, T + 3),
                 w=cv_sb[:, 4 * j + 3:4 * j + 4], s2=rot(scr2, j):
                 e.scalar_tensor_tensor(o, i, w, s2, ALU.mult, ALU.add),
                 reads=[t_u[j], t_cv, t_s2[j % 2]], writes=[t_y[j]])
            P.op("gpsimd", lambda e, o=uv(j, 0, 3), i=uv(j, T, T + 3):
                 e.tensor_copy(o, i),
                 reads=[], writes=[t_u[j]])
        P.scope = f"Bwr{c}"
        for j in range(ND):
            s = fetch_weight("wr", j, c)
            mm_group(4 + j % 2, s, D, xr_views, t_xr)
            P.op("scalar", lambda e, o=tile(tf_hf, j), p=psum[4 + j % 2]:
                 e.activation(o, p[:, :], AF.Tanh, scale=0.5),
                 reads=[t_ps[4 + j % 2]], writes=[t_tf[j]])
        P.scope = f"Bg{c}"
        for j in range(NH):
            s = fetch_weight("wg", j, c)
            mm_group(6 + j % 2, s, D, xn_views, t_xn)
            P.op("scalar", lambda e, o=tile(gg_bf, j), p=psum[6 + j % 2]:
                 e.activation(o, p[:, :], AF.Gelu),
                 reads=[t_ps[6 + j % 2]], writes=[t_gg[j]])
        for j in range(ND):
            P.op("gpsimd", lambda e, o=xnv(j, 0, 1), i=xnv(j, T, T + 1):
                 e.tensor_copy(o, i),
                 reads=[], writes=[t_xn[j]])

    def plan_CE(c, mid=None):
        def plan_c1(j):
            P.scope = f"C1_{c}_{j}"
            pi = (4, 6, 2)[j % 3]
            pr = (5, 7, 3)[j % 3]
            s = fetch_weight("lwi", j, c)
            mm_group(pi, s, H, y_views, t_y)
            P.op("scalar", lambda e, o=tile(ti_hf, j), p=psum[pi],
                 b=bi_h[:, j:j + 1]:
                 e.activation(o, p[:, :], AF.Tanh, bias=b, scale=0.5),
                 reads=[t_ps[pi], t_bih], writes=[t_ti[j]])
            s = fetch_weight("lwg", j, c)
            mm_group(pr, s, H, y_views, t_y)
            P.op("scalar", lambda e, o=tile(trg_hf, j), p=psum[pr],
                 b=bg_h[:, j:j + 1]:
                 e.activation(o, p[:, :], AF.Tanh, bias=b, scale=0.5),
                 reads=[t_ps[pr], t_bgh], writes=[t_trg[j]])

        def plan_c2(j):
            P.scope = f"C2_{c}_{j}"
            sh = s_half[:, j:j + 1]
            P.op("scalar", lambda e, o=rot3(alpha2, j), i=tile(trg_hf, j),
                 sh=sh: e.activation(o, i, AF.Exp, bias=sh, scale=sh),
                 reads=[t_trg[j], t_sh], writes=[t_alpha[j % 3]])
            P.op("scalar", lambda e, o=rot(beta2, j), i=rot3(alpha2, j):
                 e.activation(o, i, AF.Square),
                 reads=[t_alpha[j % 3]], writes=[t_beta[j % 2]])
            P.op("scalar", lambda e, o=rot(beta2, j), i=rot(beta2, j):
                 e.activation(o, i, AF.Ln, bias=cb[:, 0:1], scale=-1.0),
                 reads=[t_cb], writes=[t_beta[j % 2]])
            P.op("scalar", lambda e, o=rot(beta2, j), i=rot(beta2, j):
                 e.activation(o, i, AF.Exp, bias=cb[:, 2:3], scale=0.5),
                 reads=[t_cb], writes=[t_beta[j % 2]])
            P.op("vector", lambda e, o=rot3(igs2, j), i=tile(ti_hf, j),
                 b=rot(beta2, j):
                 e.scalar_tensor_tensor(o, i, 1.0, b, ALU.add, ALU.mult),
                 reads=[t_ti[j], t_beta[j % 2]], writes=[t_igs[j % 3]])
            pe_ = "vector" if c == 0 else "gpsimd"
            P.op(pe_, lambda e, o=rot(xbh2, j), i=rot3(igs2, j),
                 y=tile(y_bf, j): e.tensor_mul(o, i, y),
                 reads=[t_igs[j % 3], t_y[j]], writes=[t_xbh[j % 2]])
            P.op("vector", lambda e, o=rot3(igs2, j), a=rot3(alpha2, j),
                 b=rot(xbh2, j), i=hc[:, j:j + 1]:
                 e.tensor_tensor_scan(o, a, b, i, ALU.mult, ALU.add),
                 reads=[t_alpha[j % 3], t_xbh[j % 2], t_hc],
                 writes=[t_igs[j % 3]])
            P.op(pe_, lambda e, o=hc[:, j:j + 1],
                 i=rot3(igs2, j)[:, T - 1:T]: e.tensor_copy(o, i),
                 reads=[t_igs[j % 3]], writes=[t_hc])
            P.op(pe_, lambda e, o=tile(gg_bf, j), h=rot3(igs2, j):
                 e.tensor_mul(o, o, h),
                 reads=[t_igs[j % 3]], writes=[t_gg[j]])

        def plan_e1(j):
            P.scope = f"E1_{c}_{j}"
            s = fetch_weight("wk", j, c)
            mm_group(0 + j % 2, s, D, xk_views, t_xk)
            P.op("scalar", lambda e, o=rot(scr2, j), p=psum[0 + j % 2]:
                 e.activation(o, p[:, :], AF.Relu),
                 reads=[t_ps[0 + j % 2]], writes=[t_s2[j % 2]])
            P.op("scalar", lambda e, o=tile(ka_bf, j), i=rot(scr2, j):
                 e.activation(o, i, AF.Square, scale=0.7071067811865476),
                 reads=[t_s2[j % 2]], writes=[t_ka[j]])

        def plan_e2(j):
            P.scope = f"E2_{c}_{j}"
            s = fetch_weight("wv", j, c)
            mm_group(2 + j % 2, s, D, ka_views, t_ka)
            P.op("vector", lambda e, o=rot(scr2, j), i=tile(tf_hf, j),
                 p=psum[2 + j % 2]:
                 e.scalar_tensor_tensor(o, i, 1.0, p[:, :], ALU.add,
                                        ALU.mult),
                 reads=[t_tf[j], t_ps[2 + j % 2]], writes=[t_s2[j % 2]])
            P.op("vector", lambda e, o=xv(c, j), i=rot(scr2, j):
                 e.tensor_add(o, o, i),
                 reads=[t_s2[j % 2]], writes=[t_x[c % 2][j]])

        def weave(aa, bb):
            out = []
            na, nb = len(aa), len(bb)
            i = k = 0
            while i < na or k < nb:
                if i < na:
                    out.append(aa[i]); i += 1
                if k < nb:
                    out.append(bb[k]); k += 1
            return out

        for j in range(6):
            plan_c1(j)
        for unit in weave([(plan_e1, j) for j in range(ND)],
                          [(plan_c2, j) for j in range(6)]):
            unit[0](unit[1])
        if mid is not None:
            mid()
        for j in range(6, NH):
            plan_c1(j)
        for unit in weave([(plan_e2, j) for j in range(ND)],
                          [(plan_c2, j) for j in range(6, NH)]):
            unit[0](unit[1])

    def plan_D(c):
        P.scope = f"D{c}"
        col0 = c * T
        for j in range(ND):
            s = fetch_weight("wo", j, c)
            mm_group(4 + j % 4, s, H, gh_views, t_gg)
            P.op("vector", lambda e, o=xv(c, j), p=psum[4 + j % 4]:
                 e.tensor_add(o, o, p[:, :]),
                 reads=[t_ps[4 + j % 4]], writes=[t_x[c % 2][j]])
            P.op("scalar",
                 lambda e, i=xv(c, j), ap=out_d.ap()[j * 128:(j + 1) * 128,
                                                     col0:col0 + T]:
                 e.dma_start(ap, i),
                 reads=[t_x[c % 2][j]], writes=[], sem=f"od{j}", inc=16)

    for j in range(3):
        plan_casts(0, j)
    plan_A(0)

    for c in range(NCH):
        if c + 1 < NCH:
            for j in range(ND):
                P.op("scalar",
                     lambda e, o=xv(c + 1, j),
                     ap=x_d.ap()[j * 128:(j + 1) * 128,
                                 (c + 1) * T:(c + 2) * T]:
                     e.dma_start(o, ap),
                     reads=[], writes=[t_x[(c + 1) % 2][j]],
                     sem=f"xd{j}", inc=16)
        plan_B(c)

        def mid():
            for j in range(3):
                plan_casts(c + 1, j)
            plan_A(c + 1)
        plan_CE(c, mid if c + 1 < NCH else None)
        plan_D(c)

    for j in range(ND):
        P.wait("sync", f"od{j}", 16 * NCH)

    with nc.Block() as block:
        P.emit(block, sems)

    return nc


def _blk(W, nb, K):
    a = np.asarray(W, dtype=np.float32).reshape(nb, 128, K // 128, 128)
    a = a.transpose(0, 3, 2, 1)
    return np.ascontiguousarray(a.reshape(nb * 128, K))


def make_inputs(x_b, Wg, Wi, conv_w, conv_b, lru_Wi, lru_bi, lru_Wg, lru_bg,
                lam, Wo, W_r, W_k, W_v, mu_r, mu_k):
    cv = np.zeros((128, 112), np.float32)
    cv[:, 0:48] = np.asarray(conv_w, np.float32).reshape(NH, 128, 4) \
        .transpose(1, 0, 2).reshape(128, 48)
    cv[:, 48:60] = np.asarray(conv_b, np.float32).reshape(NH, 128).T
    cv[:, 60:72] = np.asarray(lam, np.float32).reshape(NH, 128).T
    cv[:, 72:84] = np.asarray(lru_bi, np.float32).reshape(NH, 128).T
    cv[:, 84:96] = np.asarray(lru_bg, np.float32).reshape(NH, 128).T
    cv[:, 96:104] = np.asarray(mu_r, np.float32).reshape(ND, 128).T
    cv[:, 104:112] = np.asarray(mu_k, np.float32).reshape(ND, 128).T
    shared = {
        "cvec": cv,
        "wg": _blk(Wg, NH, D),
        "wi": _blk(Wi, NH, D),
        "lwi": _blk(lru_Wi, NH, H),
        "lwg": _blk(lru_Wg, NH, H),
        "wo": _blk(Wo, ND, H),
        "wr": _blk(W_r, ND, D),
        "wk": _blk(W_k, ND, D),
        "wv": _blk(W_v, ND, D),
    }
    m = dict(shared)
    m["x"] = np.ascontiguousarray(np.asarray(x_b, np.float32).T)
    return m


def kernel(L0_attn_x, L0_attn_Wg, L0_attn_Wi, L0_attn_conv_w, L0_attn_conv_b,
           L0_attn_lru_Wi, L0_attn_lru_bi, L0_attn_lru_Wg, L0_attn_lru_bg,
           L0_attn_lam, L0_attn_Wo, L0_ffn_W_r, L0_ffn_W_k, L0_ffn_W_v,
           L0_ffn_mu_r, L0_ffn_mu_k):
    x = np.asarray(L0_attn_x, np.float32)
    B, S, _ = x.shape
    nc = build_program(S, 512)
    in_maps = []
    for b in range(B):
        in_maps.append(make_inputs(
            x[b], L0_attn_Wg, L0_attn_Wi, L0_attn_conv_w, L0_attn_conv_b,
            L0_attn_lru_Wi, L0_attn_lru_bi, L0_attn_lru_Wg, L0_attn_lru_bg,
            L0_attn_lam, L0_attn_Wo, L0_ffn_W_r, L0_ffn_W_k, L0_ffn_W_v,
            L0_ffn_mu_r, L0_ffn_mu_k))
    res = run_bass_kernel_spmd(nc, in_maps, core_ids=list(range(N_CORES)))
    out = np.stack([np.ascontiguousarray(np.asarray(r["out"]).T)
                    for r in res.results], axis=0)
    return out.astype(np.float32)


# version 33 (final)
# speedup vs baseline: 1.0608x; 1.0273x over previous
import numpy as np
import concourse.bass as bass
import concourse.mybir as mybir
from concourse.bass_utils import run_bass_kernel_spmd

F32 = mybir.dt.float32
BF16 = mybir.dt.bfloat16
F16 = mybir.dt.float16
AF = mybir.ActivationFunctionType
ALU = mybir.AluOpType

D, H = 1024, 1536
ND, NH = 8, 12
N_CORES = 8

ENGINES = ("tensor", "scalar", "vector", "gpsimd", "sync")
_LAST_PLOG = {}


class Tr:
    __slots__ = ("w", "r", "psum")

    def __init__(self, psum=False):
        self.w = None
        self.r = []
        self.psum = psum


class Prog:

    def __init__(self):
        self.q = {e: [] for e in ENGINES}
        self.ctr = {}
        self.synced = {e: {} for e in ENGINES}
        self.scope = ""
        self.log = {}

    def _deps(self, reads, writes):
        deps = {}

        def add(tok):
            if tok is None:
                return
            s, v = tok
            if deps.get(s, 0) < v:
                deps[s] = v

        for t in reads:
            add(t.w)
        for t in writes:
            add(t.w)
            for tok in t.r:
                add(tok)
        return deps

    def op(self, eng, fn, reads=(), writes=(), sem=None, inc=1):
        deps = self._deps(reads, writes)
        if eng in ("scalar", "vector"):
            other = "vector" if eng == "scalar" else "scalar"
            for t in reads:
                if getattr(t, "psum", False):
                    for (s0, v0) in t.r:
                        if s0 == other and deps.get(s0, 0) < v0:
                            deps[s0] = v0
        if inc == 16 and sem is not None:
            prev = self.ctr.get(sem, 0)
            if prev and deps.get(sem, 0) < prev:
                deps[sem] = prev
        waits = []
        for s in sorted(deps):
            v = deps[s]
            if eng == "tensor" and s == "tensor":
                continue
            if self.synced[eng].get(s, 0) < v:
                waits.append((s, v))
                self.synced[eng][s] = v
        s = sem if sem is not None else eng
        self.ctr[s] = self.ctr.get(s, 0) + inc
        tok = (s, self.ctr[s])
        self.log[tok] = self.scope
        self.q[eng].append((waits, fn, s, inc))
        for t in reads:
            t.r.append(tok)
        for t in writes:
            t.w = tok
            t.r = []
        return tok

    def wait(self, eng, sem, val):
        self.q[eng].append(([(sem, val)], None, None, 0))
        self.synced[eng][sem] = max(self.synced[eng].get(sem, 0), val)

    def emit(self, block, sems):
        for eng in ENGINES:
            entries = self.q[eng]

            def make_body(entries=entries):
                def body(e):
                    for waits, fn, s, inc in entries:
                        for ws, wv in waits:
                            e.wait_ge(sems[ws], wv)
                        if fn is not None:
                            ins = fn(e)
                            ins.then_inc(sems[s], inc)
                return body

            getattr(block, eng)(make_body())


def build_program(S, T):
    NCH = S // T
    assert S % T == 0
    nc = bass.Bass("TRN2", target_bir_lowering=False)

    x_d = nc.dram_tensor("x", [D, S], F32, kind="ExternalInput")
    out_d = nc.dram_tensor("out", [D, S], F32, kind="ExternalOutput")
    cvec_d = nc.dram_tensor("cvec", [128, 112], F32, kind="ExternalInput")
    w_in = {}
    w_scr = {}
    W_GEOM = {
        "wi": (NH, D), "wr": (ND, D), "wg": (NH, D),
        "lwi": (NH, H), "lwg": (NH, H),
        "wk": (ND, D), "wv": (ND, D), "wo": (ND, H),
    }
    for nm, (nb, K) in W_GEOM.items():
        w_in[nm] = nc.dram_tensor(nm, [nb * 128, K], F32, kind="ExternalInput")
        w_scr[nm] = nc.dram_tensor(nm + "_s", [nb * 128, K], BF16, kind="Internal")

    from contextlib import ExitStack
    stack = ExitStack()

    def sb(name, shape, dt):
        return stack.enter_context(nc.sbuf_tensor(name, shape, dt))

    x_sb = sb("x_sb", [128, 2 * ND * T], F32)
    xn_bf = sb("xn_bf", [128, ND * (T + 1)], BF16)
    u_bf = sb("u_bf", [128, NH * (T + 3)], BF16)
    y_bf = sb("y_bf", [128, NH * T], BF16)
    gg_bf = sb("gg_bf", [128, NH * T], BF16)
    ti_hf = sb("ti_hf", [128, NH * T], F16)
    trg_hf = sb("trg_hf", [128, NH * T], F16)
    tf_hf = sb("tf_hf", [128, ND * T], F16)
    xr_bf = sb("xr_bf", [128, ND * T], BF16)
    xk_bf = sb("xk_bf", [128, ND * T], BF16)
    ka_bf = sb("ka_bf", [128, ND * T], BF16)
    alpha2 = sb("alpha2", [128, 3 * T], F32)
    xbh2 = sb("xbh2", [128, 2 * T], F32)
    beta2 = sb("beta2", [128, 2 * T], F32)
    igs2 = sb("igs2", [128, 3 * T], F32)
    scr2 = sb("scr2", [128, 2 * T], F32)
    diff2 = sb("diff2", [128, 2 * T], F32)
    xbf4 = sb("xbf4", [128, 3 * T], BF16)
    sq4 = sb("sq4", [128, 3 * T], BF16)
    rows = sb("rows", [1, 4 * T], F32)
    cv_sb = sb("cv_sb", [128, 112], F32)
    s_half = sb("s_half", [128, NH], F32)
    bi_h = sb("bi_h", [128, NH], F32)
    bg_h = sb("bg_h", [128, NH], F32)
    hc = sb("hc", [128, NH], F32)
    ones_bf = sb("ones_bf", [128, 1], BF16)
    ones_r = sb("ones_r", [1, 128], F32)
    cb = sb("cb", [128, 3], F32)
    rb_sb = sb("rb_sb", [128, T], F32)
    RING = 9
    ring = sb("ring", [128, RING * H], BF16)
    ring_f32 = ring.bitcast(F32)

    psum = [stack.enter_context(nc.psum_tensor(f"pb{i}", [128, T], F32))
            for i in range(8)]

    sem_names = (["tensor", "scalar", "vector", "gpsimd", "cd0", "cd1", "cd2", "cd3"]
                 + [f"wd{s}" for s in range(RING)] + [f"sd{s}" for s in range(RING)]
                 + [f"xd{j}" for j in range(ND)] + [f"od{j}" for j in range(ND)])
    sems = {n: stack.enter_context(nc.semaphore("sem_" + n)) for n in sem_names}

    P = Prog()
    global _LAST_PLOG
    _LAST_PLOG = P.log

    t_x = [[Tr() for _ in range(ND)], [Tr() for _ in range(ND)]]
    t_xn = [Tr() for _ in range(ND)]
    t_u = [Tr() for _ in range(NH)]
    t_y = [Tr() for _ in range(NH)]
    t_gg = [Tr() for _ in range(NH)]
    t_ti = [Tr() for _ in range(NH)]
    t_trg = [Tr() for _ in range(NH)]
    t_tf = [Tr() for _ in range(ND)]
    t_xr = [Tr() for _ in range(ND)]
    t_xk = [Tr() for _ in range(ND)]
    t_ka = [Tr() for _ in range(ND)]
    t_alpha = [Tr() for _ in range(3)]
    t_xbh = [Tr() for _ in range(2)]
    t_beta = [Tr() for _ in range(2)]
    t_igs = [Tr() for _ in range(3)]
    t_s2 = [Tr() for _ in range(2)]
    t_diff = [Tr() for _ in range(2)]
    t_xbf = [Tr() for _ in range(3)]
    t_sq = [Tr() for _ in range(3)]
    t_rows = Tr()
    t_cv = Tr()
    t_sh = Tr()
    t_bih = Tr()
    t_bgh = Tr()
    t_hc = Tr()
    t_ones = Tr()
    t_onesr = Tr()
    t_ring = [Tr() for _ in range(RING)]
    t_cvt = Tr()
    t_ps = [Tr(psum=True) for _ in range(8)]
    t_scr = {nm: [Tr() for _ in range(nb)] for nm, (nb, K) in W_GEOM.items()}

    def xv(cc, j):
        b = ((cc % 2) * ND + j) * T
        return x_sb[:, b:b + T]

    def xnv(j, lo, hi):
        b = j * (T + 1)
        return xn_bf[:, b + lo: b + hi]

    def uv(j, lo, hi):
        b = j * (T + 3)
        return u_bf[:, b + lo: b + hi]

    def tile(t_buf, j):
        return t_buf[:, j * T:(j + 1) * T]

    def rot(buf, i):
        return buf[:, (i % 2) * T:((i % 2) + 1) * T]

    def rot4(buf, i):
        return buf[:, (i % 4) * T:((i % 4) + 1) * T]

    def rot3(buf, i):
        return buf[:, (i % 3) * T:((i % 3) + 1) * T]

    mean_row = rows[0:1, 0:T]
    msq_row = rows[0:1, T:2 * T]
    var_row = rows[0:1, 2 * T:3 * T]
    rstd_row = rows[0:1, 3 * T:4 * T]
    t_mean = Tr(); t_msq = Tr(); t_var = Tr(); t_rstd = Tr()
    t_rb = Tr()

    def ring_view(s, K):
        return ring[:, s * H: s * H + K]

    ring_state = {"i": 0, "q": 0}

    def next_slot():
        s = ring_state["i"] % RING
        ring_state["i"] += 1
        return s

    cvst = {"q": 0, "d": 0, "st": 0}

    def fetch_weight(nm, j, c):
        nb, K = W_GEOM[nm]
        if c == 0:
            q = cvst["q"] % 3
            cvst["q"] += 1
            s = 6 + cvst["d"] % 3
            cvst["d"] += 1
            stage = ring_f32[:, q * H:q * H + K]
            rsv = ring_view(s, K)
            P.op("sync",
                 lambda e, o=stage, ap=w_in[nm].ap()[j * 128:(j + 1) * 128, :]:
                 e.dma_start(o, ap),
                 reads=[], writes=[t_ring[2 * q], t_ring[2 * q + 1]],
                 sem=f"cd{q}", inc=16)
            P.op("vector", lambda e, o=rsv, i=stage: e.tensor_copy(o, i),
                 reads=[t_ring[2 * q], t_ring[2 * q + 1]], writes=[t_ring[s]])
            P.op("gpsimd",
                 lambda e, o=w_scr[nm].ap()[j * 128:(j + 1) * 128, :], i=rsv:
                 e.dma_start(o, i),
                 reads=[t_ring[s]], writes=[t_scr[nm][j]], sem=f"sd{s}", inc=16)
        else:
            s = next_slot()
            rsv = ring_view(s, K)
            P.op("sync",
                 lambda e, o=rsv, ap=w_scr[nm].ap()[j * 128:(j + 1) * 128, :]:
                 e.dma_start(o, ap),
                 reads=[t_scr[nm][j]], writes=[t_ring[s]], sem=f"wd{s}", inc=16)
        return s

    def mm_group(ps_i, slot, K, rhs_views, rhs_trs, extra_reads=()):
        nk = K // 128
        rsv = ring_view(slot, K)
        for kh in range(nk):
            def fn(e, rsv=rsv, kh=kh, v=rhs_views[kh], p=psum[ps_i],
                   st=(kh == 0), sp=(kh == nk - 1)):
                return e.matmul(p[:, :], rsv[:, kh * 128:(kh + 1) * 128], v,
                                start=st, stop=sp)
            extra = list(extra_reads) if kh == 0 else []
            P.op("tensor", fn,
                 reads=[t_ring[slot], rhs_trs[kh]] + extra,
                 writes=[t_ps[ps_i]])

    P.op("sync", lambda e: e.dma_start(cv_sb[:], cvec_d.ap()[:, :]),
         reads=[], writes=[t_cv], sem="cd0", inc=16)
    for j in range(ND):
        P.op("sync",
             lambda e, o=xv(0, j), ap=x_d.ap()[j * 128:(j + 1) * 128, 0:T]:
             e.dma_start(o, ap),
             reads=[], writes=[t_x[0][j]], sem=f"xd{j}", inc=16)
    P.op("vector", lambda e: e.memset(ones_bf[:], 1.0), writes=[t_ones])
    P.op("vector", lambda e: e.memset(ones_r[:], 1.0), writes=[t_onesr])
    t_cb = Tr()
    P.op("vector", lambda e: e.memset(cb[:, 0:1], 1.000001), writes=[t_cb])
    P.op("vector", lambda e: e.memset(cb[:, 1:2], 1e-5), writes=[t_cb])
    P.op("vector", lambda e: e.memset(cb[:, 2:3], -0.6931471805599453), writes=[t_cb])
    P.op("vector", lambda e: e.memset(hc[:], 0.0), writes=[t_hc])
    P.op("vector", lambda e: e.memset(xn_bf[:], 0.0), writes=t_xn)
    P.op("vector", lambda e: e.memset(u_bf[:], 0.0), writes=t_u)
    lam_v = cv_sb[:, 60:72]
    P.op("scalar", lambda e: e.activation(s_half[:], lam_v, AF.Exp),
         reads=[t_cv], writes=[t_sh])
    P.op("scalar", lambda e: e.activation(s_half[:], s_half[:], AF.Ln, bias=1.0),
         reads=[], writes=[t_sh])
    P.op("scalar", lambda e: e.activation(s_half[:], s_half[:], AF.Copy, scale=-4.0),
         reads=[], writes=[t_sh])
    P.op("scalar", lambda e: e.activation(bi_h[:], cv_sb[:, 72:84], AF.Copy, scale=0.5),
         reads=[t_cv], writes=[t_bih])
    P.op("scalar", lambda e: e.activation(bg_h[:], cv_sb[:, 84:96], AF.Copy, scale=0.5),
         reads=[t_cv], writes=[t_bgh])

    xn_views = tuple(xnv(kh, 1, T + 1) for kh in range(ND))
    y_views = tuple(tile(y_bf, kh) for kh in range(NH))
    xr_views = tuple(tile(xr_bf, kh) for kh in range(ND))
    xk_views = tuple(tile(xk_bf, kh) for kh in range(ND))
    ka_views = tuple(tile(ka_bf, kh) for kh in range(ND))
    gh_views = tuple(tile(gg_bf, kh) for kh in range(NH))

    def plan_casts(cc, j):
        P.op("vector", lambda e, o=rot3(xbf4, j), i=xv(cc, j):
             e.tensor_copy(o, i),
             reads=[t_x[cc % 2][j]], writes=[t_xbf[j % 3]])
        P.op("scalar", lambda e, o=rot3(sq4, j), i=xv(cc, j):
             e.activation(o, i, AF.Square),
             reads=[t_x[cc % 2][j]], writes=[t_sq[j % 3]])

    def plan_A(cc):
        P.scope = f"A{cc}"
        """LN stats + normalize + ffn token-shift prep for chunk cc; planned
        before the previous chunk's D stage so it overlaps it."""
        tx = t_x[cc % 2]
        for j in range(ND):
            if j >= 3:
                plan_casts(cc, j)
            P.op("tensor", lambda e, r=rot3(xbf4, j), j=j:
                 e.matmul(psum[0][0:1, :], ones_bf[:, :], r,
                          start=(j == 0), stop=(j == ND - 1)),
                 reads=[t_ones, t_xbf[j % 3]], writes=[t_ps[0]])
            P.op("tensor", lambda e, r=rot3(sq4, j), j=j:
                 e.matmul(psum[1][0:1, :], ones_bf[:, :], r,
                          start=(j == 0), stop=(j == ND - 1)),
                 reads=[t_ones, t_sq[j % 3]], writes=[t_ps[1]])
        P.op("scalar", lambda e: e.activation(mean_row, psum[0][0:1, :],
                                              AF.Copy, scale=1.0 / D),
             reads=[t_ps[0]], writes=[t_mean])
        P.op("scalar", lambda e: e.activation(msq_row, psum[0][0:1, :],
                                              AF.Square, scale=1.0 / D),
             reads=[t_ps[0]], writes=[t_msq])
        P.op("scalar", lambda e: e.activation(var_row, psum[1][0:1, :],
                                              AF.Copy, scale=1.0 / D),
             reads=[t_ps[1]], writes=[t_var])
        P.op("vector", lambda e: e.tensor_sub(var_row, var_row, msq_row),
             reads=[t_msq], writes=[t_var])
        P.op("scalar", lambda e: e.activation(var_row, var_row, AF.Ln,
                                              bias=cb[0:1, 1:2]),
             reads=[t_cb], writes=[t_var])
        P.op("scalar", lambda e: e.activation(rstd_row, var_row, AF.Exp,
                                              scale=-0.5),
             reads=[t_var], writes=[t_rstd])
        P.op("tensor", lambda e: e.matmul(psum[0][:, :], ones_r[:, :],
                                          mean_row, start=True, stop=True),
             reads=[t_onesr, t_mean], writes=[t_ps[0]])
        P.op("tensor", lambda e: e.matmul(psum[1][:, :], ones_r[:, :],
                                          rstd_row, start=True, stop=True),
             reads=[t_onesr, t_rstd], writes=[t_ps[1]])
        P.op("scalar", lambda e: e.activation(rb_sb[:], psum[1][:, :],
                                              AF.Copy),
             reads=[t_ps[1]], writes=[t_rb])
        for j in range(ND):
            P.op("vector", lambda e, o=rot(diff2, j), i=xv(cc, j):
                 e.tensor_sub(o, i, psum[0][:, :]),
                 reads=[tx[j], t_ps[0]], writes=[t_diff[j % 2]])
            if j % 2 == 0:
                P.op("vector", lambda e, o=xnv(j, 1, T + 1), i=rot(diff2, j):
                     e.tensor_mul(o, i, psum[1][:, :]),
                     reads=[t_diff[j % 2], t_ps[1]], writes=[t_xn[j]])
            else:
                P.op("vector" if cc == 0 else "gpsimd",
                     lambda e, o=xnv(j, 1, T + 1), i=rot(diff2, j):
                     e.tensor_mul(o, i, rb_sb[:]),
                     reads=[t_diff[j % 2], t_rb], writes=[t_xn[j]])
        for j in range(ND):
            P.op("vector", lambda e, o=rot(diff2, j), a=xnv(j, 1, T + 1),
                 b=xnv(j, 0, T): e.tensor_sub(o, a, b),
                 reads=[t_xn[j]], writes=[t_diff[j % 2]])
            P.op("vector", lambda e, o=tile(xr_bf, j), d=rot(diff2, j),
                 m=cv_sb[:, 96 + j:97 + j], p=xnv(j, 0, T):
                 e.scalar_tensor_tensor(o, d, m, p, ALU.mult, ALU.add),
                 reads=[t_diff[j % 2], t_cv, t_xn[j]], writes=[t_xr[j]])
            P.op("vector", lambda e, o=tile(xk_bf, j), d=rot(diff2, j),
                 m=cv_sb[:, 104 + j:105 + j], p=xnv(j, 0, T):
                 e.scalar_tensor_tensor(o, d, m, p, ALU.mult, ALU.add),
                 reads=[t_diff[j % 2], t_cv, t_xn[j]], writes=[t_xk[j]])

    def plan_B(c):
        P.scope = f"Bu{c}"
        for j in range(NH):
            pu = (2, 3, 0, 1)[j % 4]
            s = fetch_weight("wi", j, c)
            mm_group(pu, s, D, xn_views, t_xn)
            P.op("scalar", lambda e, o=uv(j, 3, T + 3), p=psum[pu]:
                 e.activation(o, p[:, :], AF.Copy),
                 reads=[t_ps[pu]], writes=[t_u[j]])
        P.scope = f"conv{c}"
        for j in range(NH):
            P.op("vector", lambda e, o=rot(scr2, j), i=uv(j, 0, T),
                 w=cv_sb[:, 4 * j:4 * j + 1], b=cv_sb[:, 48 + j:49 + j]:
                 e.tensor_scalar(o, i, w, b, ALU.mult, ALU.add),
                 reads=[t_u[j], t_cv], writes=[t_s2[j % 2]])
            for tap in (1, 2):
                P.op("vector", lambda e, o=rot(scr2, j), i=uv(j, tap, T + tap),
                     w=cv_sb[:, 4 * j + tap:4 * j + tap + 1], s2=rot(scr2, j):
                     e.scalar_tensor_tensor(o, i, w, s2, ALU.mult, ALU.add),
                     reads=[t_u[j], t_cv], writes=[t_s2[j % 2]])
            P.op("vector", lambda e, o=tile(y_bf, j), i=uv(j, 3, T + 3),
                 w=cv_sb[:, 4 * j + 3:4 * j + 4], s2=rot(scr2, j):
                 e.scalar_tensor_tensor(o, i, w, s2, ALU.mult, ALU.add),
                 reads=[t_u[j], t_cv, t_s2[j % 2]], writes=[t_y[j]])
            P.op("gpsimd", lambda e, o=uv(j, 0, 3), i=uv(j, T, T + 3):
                 e.tensor_copy(o, i),
                 reads=[], writes=[t_u[j]])
        P.scope = f"Bwr{c}"
        for j in range(ND):
            s = fetch_weight("wr", j, c)
            mm_group(4 + j % 2, s, D, xr_views, t_xr)
            P.op("scalar", lambda e, o=tile(tf_hf, j), p=psum[4 + j % 2]:
                 e.activation(o, p[:, :], AF.Tanh, scale=0.5),
                 reads=[t_ps[4 + j % 2]], writes=[t_tf[j]])
        P.scope = f"Bg{c}"
        for j in range(NH):
            s = fetch_weight("wg", j, c)
            mm_group(6 + j % 2, s, D, xn_views, t_xn)
            P.op("scalar", lambda e, o=tile(gg_bf, j), p=psum[6 + j % 2]:
                 e.activation(o, p[:, :], AF.Gelu),
                 reads=[t_ps[6 + j % 2]], writes=[t_gg[j]])
        for j in range(ND):
            P.op("gpsimd", lambda e, o=xnv(j, 0, 1), i=xnv(j, T, T + 1):
                 e.tensor_copy(o, i),
                 reads=[], writes=[t_xn[j]])

    def plan_CE(c, mid=None):
        def plan_c1(j):
            P.scope = f"C1_{c}_{j}"
            pi = (4, 6, 2)[j % 3]
            pr = (5, 7, 3)[j % 3]
            s = fetch_weight("lwi", j, c)
            mm_group(pi, s, H, y_views, t_y)
            P.op("scalar", lambda e, o=tile(ti_hf, j), p=psum[pi],
                 b=bi_h[:, j:j + 1]:
                 e.activation(o, p[:, :], AF.Tanh, bias=b, scale=0.5),
                 reads=[t_ps[pi], t_bih], writes=[t_ti[j]])
            s = fetch_weight("lwg", j, c)
            mm_group(pr, s, H, y_views, t_y)
            P.op("scalar", lambda e, o=tile(trg_hf, j), p=psum[pr],
                 b=bg_h[:, j:j + 1]:
                 e.activation(o, p[:, :], AF.Tanh, bias=b, scale=0.5),
                 reads=[t_ps[pr], t_bgh], writes=[t_trg[j]])

        def plan_c2(j):
            P.scope = f"C2_{c}_{j}"
            sh = s_half[:, j:j + 1]
            P.op("scalar", lambda e, o=rot3(alpha2, j), i=tile(trg_hf, j),
                 sh=sh: e.activation(o, i, AF.Exp, bias=sh, scale=sh),
                 reads=[t_trg[j], t_sh], writes=[t_alpha[j % 3]])
            P.op("scalar", lambda e, o=rot(beta2, j), i=rot3(alpha2, j):
                 e.activation(o, i, AF.Square),
                 reads=[t_alpha[j % 3]], writes=[t_beta[j % 2]])
            P.op("scalar", lambda e, o=rot(beta2, j), i=rot(beta2, j):
                 e.activation(o, i, AF.Ln, bias=cb[:, 0:1], scale=-1.0),
                 reads=[t_cb], writes=[t_beta[j % 2]])
            P.op("scalar", lambda e, o=rot(beta2, j), i=rot(beta2, j):
                 e.activation(o, i, AF.Exp, bias=cb[:, 2:3], scale=0.5),
                 reads=[t_cb], writes=[t_beta[j % 2]])
            P.op("vector", lambda e, o=rot3(igs2, j), i=tile(ti_hf, j),
                 b=rot(beta2, j):
                 e.scalar_tensor_tensor(o, i, 1.0, b, ALU.add, ALU.mult),
                 reads=[t_ti[j], t_beta[j % 2]], writes=[t_igs[j % 3]])
            pe_ = "vector" if c == 0 else "gpsimd"
            P.op(pe_, lambda e, o=rot(xbh2, j), i=rot3(igs2, j),
                 y=tile(y_bf, j): e.tensor_mul(o, i, y),
                 reads=[t_igs[j % 3], t_y[j]], writes=[t_xbh[j % 2]])
            P.op("vector", lambda e, o=rot3(igs2, j), a=rot3(alpha2, j),
                 b=rot(xbh2, j), i=hc[:, j:j + 1]:
                 e.tensor_tensor_scan(o, a, b, i, ALU.mult, ALU.add),
                 reads=[t_alpha[j % 3], t_xbh[j % 2], t_hc],
                 writes=[t_igs[j % 3]])
            P.op(pe_, lambda e, o=hc[:, j:j + 1],
                 i=rot3(igs2, j)[:, T - 1:T]: e.tensor_copy(o, i),
                 reads=[t_igs[j % 3]], writes=[t_hc])
            P.op(pe_, lambda e, o=tile(gg_bf, j), h=rot3(igs2, j):
                 e.tensor_mul(o, o, h),
                 reads=[t_igs[j % 3]], writes=[t_gg[j]])

        def plan_e1(j):
            P.scope = f"E1_{c}_{j}"
            s = fetch_weight("wk", j, c)
            mm_group(0 + j % 2, s, D, xk_views, t_xk)
            P.op("scalar", lambda e, o=rot(scr2, j), p=psum[0 + j % 2]:
                 e.activation(o, p[:, :], AF.Relu),
                 reads=[t_ps[0 + j % 2]], writes=[t_s2[j % 2]])
            P.op("scalar", lambda e, o=tile(ka_bf, j), i=rot(scr2, j):
                 e.activation(o, i, AF.Square, scale=0.7071067811865476),
                 reads=[t_s2[j % 2]], writes=[t_ka[j]])

        def plan_e2(j):
            P.scope = f"E2_{c}_{j}"
            s = fetch_weight("wv", j, c)
            mm_group(2 + j % 2, s, D, ka_views, t_ka)
            P.op("vector", lambda e, o=rot(scr2, j), i=tile(tf_hf, j),
                 p=psum[2 + j % 2]:
                 e.scalar_tensor_tensor(o, i, 1.0, p[:, :], ALU.add,
                                        ALU.mult),
                 reads=[t_tf[j], t_ps[2 + j % 2]], writes=[t_s2[j % 2]])
            P.op("vector", lambda e, o=xv(c, j), i=rot(scr2, j):
                 e.tensor_add(o, o, i),
                 reads=[t_s2[j % 2]], writes=[t_x[c % 2][j]])

        def weave(aa, bb):
            out = []
            na, nb = len(aa), len(bb)
            i = k = 0
            while i < na or k < nb:
                if i < na:
                    out.append(aa[i]); i += 1
                if k < nb:
                    out.append(bb[k]); k += 1
            return out

        for j in range(6):
            plan_c1(j)
        for unit in weave([(plan_e1, j) for j in range(ND)],
                          [(plan_c2, j) for j in range(6)]):
            unit[0](unit[1])
        if mid is not None:
            mid()
        for j in range(6, NH):
            plan_c1(j)
        for unit in weave([(plan_e2, j) for j in range(ND)],
                          [(plan_c2, j) for j in range(6, NH)]):
            unit[0](unit[1])

    def plan_D(c):
        P.scope = f"D{c}"
        col0 = c * T
        for j in range(ND):
            s = fetch_weight("wo", j, c)
            mm_group(4 + j % 4, s, H, gh_views, t_gg)
            P.op("vector", lambda e, o=xv(c, j), p=psum[4 + j % 4]:
                 e.tensor_add(o, o, p[:, :]),
                 reads=[t_ps[4 + j % 4]], writes=[t_x[c % 2][j]])
            P.op("scalar",
                 lambda e, i=xv(c, j), ap=out_d.ap()[j * 128:(j + 1) * 128,
                                                     col0:col0 + T]:
                 e.dma_start(ap, i),
                 reads=[t_x[c % 2][j]], writes=[], sem=f"od{j}", inc=16)

    for j in range(3):
        plan_casts(0, j)
    plan_A(0)

    for c in range(NCH):
        if c + 1 < NCH:
            for j in range(ND):
                P.op("scalar",
                     lambda e, o=xv(c + 1, j),
                     ap=x_d.ap()[j * 128:(j + 1) * 128,
                                 (c + 1) * T:(c + 2) * T]:
                     e.dma_start(o, ap),
                     reads=[], writes=[t_x[(c + 1) % 2][j]],
                     sem=f"xd{j}", inc=16)
        plan_B(c)

        def mid():
            for j in range(3):
                plan_casts(c + 1, j)
            plan_A(c + 1)
        plan_CE(c, mid if c + 1 < NCH else None)
        plan_D(c)

    for j in range(ND):
        P.wait("sync", f"od{j}", 16 * NCH)

    with nc.Block() as block:
        P.emit(block, sems)

    return nc


def _blk(W, nb, K):
    a = np.asarray(W, dtype=np.float32).reshape(nb, 128, K // 128, 128)
    a = a.transpose(0, 3, 2, 1)
    return np.ascontiguousarray(a.reshape(nb * 128, K))


def make_inputs(x_b, Wg, Wi, conv_w, conv_b, lru_Wi, lru_bi, lru_Wg, lru_bg,
                lam, Wo, W_r, W_k, W_v, mu_r, mu_k):
    cv = np.zeros((128, 112), np.float32)
    cv[:, 0:48] = np.asarray(conv_w, np.float32).reshape(NH, 128, 4) \
        .transpose(1, 0, 2).reshape(128, 48)
    cv[:, 48:60] = np.asarray(conv_b, np.float32).reshape(NH, 128).T
    cv[:, 60:72] = np.asarray(lam, np.float32).reshape(NH, 128).T
    cv[:, 72:84] = np.asarray(lru_bi, np.float32).reshape(NH, 128).T
    cv[:, 84:96] = np.asarray(lru_bg, np.float32).reshape(NH, 128).T
    cv[:, 96:104] = np.asarray(mu_r, np.float32).reshape(ND, 128).T
    cv[:, 104:112] = np.asarray(mu_k, np.float32).reshape(ND, 128).T
    shared = {
        "cvec": cv,
        "wg": _blk(Wg, NH, D),
        "wi": _blk(Wi, NH, D),
        "lwi": _blk(lru_Wi, NH, H),
        "lwg": _blk(lru_Wg, NH, H),
        "wo": _blk(Wo, ND, H),
        "wr": _blk(W_r, ND, D),
        "wk": _blk(W_k, ND, D),
        "wv": _blk(W_v, ND, D),
    }
    m = dict(shared)
    m["x"] = np.ascontiguousarray(np.asarray(x_b, np.float32).T)
    return m


def kernel(L0_attn_x, L0_attn_Wg, L0_attn_Wi, L0_attn_conv_w, L0_attn_conv_b,
           L0_attn_lru_Wi, L0_attn_lru_bi, L0_attn_lru_Wg, L0_attn_lru_bg,
           L0_attn_lam, L0_attn_Wo, L0_ffn_W_r, L0_ffn_W_k, L0_ffn_W_v,
           L0_ffn_mu_r, L0_ffn_mu_k):
    x = np.asarray(L0_attn_x, np.float32)
    B, S, _ = x.shape
    nc = build_program(S, 512)
    in_maps = []
    for b in range(B):
        in_maps.append(make_inputs(
            x[b], L0_attn_Wg, L0_attn_Wi, L0_attn_conv_w, L0_attn_conv_b,
            L0_attn_lru_Wi, L0_attn_lru_bi, L0_attn_lru_Wg, L0_attn_lru_bg,
            L0_attn_lam, L0_attn_Wo, L0_ffn_W_r, L0_ffn_W_k, L0_ffn_W_v,
            L0_ffn_mu_r, L0_ffn_mu_k))
    res = run_bass_kernel_spmd(nc, in_maps, core_ids=list(range(N_CORES)))
    out = np.stack([np.ascontiguousarray(np.asarray(r["out"]).T)
                    for r in res.results], axis=0)
    return out.astype(np.float32)
